# version 01
# baseline (speedup 1.0000x reference)
import math
import contextlib
import numpy as np
import ml_dtypes
import concourse.bass as bass
import concourse.mybir as mybir
from concourse.bass_utils import run_bass_kernel_spmd

F32 = mybir.dt.float32
BF16 = mybir.dt.bfloat16
AF = mybir.ActivationFunctionType
ALU = mybir.AluOpType
AX = mybir.AxisListType

E = 1024
H = 16
HD = 64
NBLK = 16
HID = 4096
NCORES = 8
C_SHIFT = 12.0
BIG = float(2 ** 30)
LN2 = math.log(2.0)

ENGINES = ("pe", "act", "dve", "gp", "sp")


class Op:
    __slots__ = ("eng", "fn", "deps", "sem_kind", "inc_needed", "cum")

    def __init__(self, eng, fn, deps, sem_kind):
        self.eng = eng
        self.fn = fn
        self.deps = [d for d in deps if d is not None]
        self.sem_kind = sem_kind
        self.inc_needed = False
        self.cum = None


class Prog:
    def __init__(self):
        self.ops = []

    def add(self, eng, fn, deps=()):
        op = Op(eng, fn, deps, "eng")
        self.ops.append(op)
        return op

    def dma(self, queue, out_ap, in_ap, deps=()):
        op = Op(queue, lambda e, o=out_ap, i=in_ap: e.dma_start(out=o, in_=i),
                deps, "dma")
        self.ops.append(op)
        return op

    def emit_into(self, block, sems, sim_safe=False, paranoid=False):
        if sim_safe or paranoid:
            if sim_safe or paranoid is True:
                chain_engs = ENGINES
            elif paranoid == "soft":
                chain_engs = ("act", "dve")
            elif paranoid == "pedve":
                chain_engs = ("act", "dve", "pe")
            else:
                chain_engs = ("act", "dve")
            last = {}
            for op in self.ops:
                key = op.eng
                if key not in chain_engs:
                    continue
                if key in last:
                    op.deps.append(last[key])
                last[key] = op
        for op in self.ops:
            for d in op.deps:
                d.inc_needed = True
            if op.sem_kind == "dma":
                op.inc_needed = True
        cums = {}
        for op in self.ops:
            key = (op.eng, op.sem_kind)
            c = cums.get(key, 0)
            if op.inc_needed:
                c += 16 if op.sem_kind == "dma" else 1
            cums[key] = c
            op.cum = c
        per_eng = {e: [] for e in ENGINES}
        for op in self.ops:
            per_eng[op.eng].append(op)

        def runner(eng_name):
            def run(engine):
                waited = {}
                for op in per_eng[eng_name]:
                    for d in op.deps:
                        if (d.eng == eng_name and d.sem_kind == "eng"
                                and not (sim_safe or paranoid)):
                            continue
                        if (paranoid in ("soft", "pedve") and not sim_safe
                                and d.eng == eng_name and d.sem_kind == "eng"
                                and eng_name not in (("act", "dve", "pe") if paranoid == "pedve" else ("act", "dve"))):
                            continue
                        key = (d.eng, d.sem_kind)
                        if waited.get(key, -1) >= d.cum:
                            continue
                        engine.wait_ge(sems[key], d.cum)
                        waited[key] = d.cum
                    instr = op.fn(engine)
                    if op.inc_needed:
                        assert instr is not None
                        instr.then_inc(sems[(op.eng, op.sem_kind)],
                                       16 if op.sem_kind == "dma" else 1)
                    if sim_safe and op.sem_kind == "dma":
                        engine.wait_ge(sems[(op.eng, "dma")], op.cum)
            return run

        block.tensor(runner("pe"))
        block.scalar(runner("act"))
        block.vector(runner("dve"))
        block.gpsimd(runner("gp"))
        block.sync(runner("sp"))


def build_program(P2C=2, T=4096, dbg=False, sim_safe=False, paranoid=False):
    nQ = H // 4
    nTC = T // 128
    nTB = T // 512
    nKC = T // 128
    nQB = T // 512
    nDELTA = max(nKC - 4, 1)

    nc = bass.Bass("TRN2", target_bir_lowering=False)

    def din(name, shape, dt=F32):
        return nc.dram_tensor(name, shape, dt, kind="ExternalInput").ap()

    x_in = din("x", [P2C, T, E])
    wqT = din("wqT", [E, E]); wkT = din("wkT", [E, E])
    wvT = din("wvT", [E, E]); wgT = din("wgT", [E, E]); woT = din("woT", [E, E])
    w1T = din("w1T", [E, HID]); w2T = din("w2T", [HID, E])
    routT = din("routT", [E, NBLK]); wdT = din("wdT", [E, H])
    bqEd = din("bqE", [128, nQ]); bqOd = din("bqO", [128, nQ])
    bkEd = din("bkE", [128, nQ]); bkOd = din("bkO", [128, nQ])
    browsd = din("brows", [1, 4 * E])
    bg_cd = din("bgc", [128, 8]); b1_cd = din("b1c", [128, HID // 128])
    gam_cd = din("gamc", [128, 8]); bet_cd = din("betc", [128, 8])
    rowsmd = din("rowsm", [1, 32])
    bd_cd = din("bdc", [H, 1]); rb_cd = din("rbc", [NBLK, 1])
    identFd = din("identF", [128, 128])
    ropetabd = din("ropetab", [128, 4 * T], BF16)
    tdeltad = din("tdelta", [128, 4 * 512])
    iotaqd = din("iotaq", [16, 512])
    iota28d = din("iota28", [128, nDELTA])
    ind16d = din("ind16", [16, 16 * 128], BF16)
    hm2d = din("hm2", [128, 2]); hm2Td = din("hm2T", [2, 128])
    onesfd = din("onesf", [1, 128])
    constfd = din("constf", [128, 3])

    out = nc.dram_tensor("out", [P2C, T, E], F32, kind="ExternalOutput").ap()
    ks = "ExternalOutput" if dbg else "Internal"
    xnTs = nc.dram_tensor("xnTs", [P2C, E, T], BF16, kind=ks).ap()
    goTs = nc.dram_tensor("goTs", [P2C, E, T], BF16, kind=ks).ap()
    maskD = nc.dram_tensor("maskD", [P2C, 16, T], BF16, kind=ks).ap()
    logitD = nc.dram_tensor("logitD", [P2C, 16, T], F32, kind=ks).ap()
    w1b = nc.dram_tensor("w1b", [E, HID], BF16, kind="Internal").ap()
    w2b = nc.dram_tensor("w2b", [HID, E], BF16, kind="Internal").ap()
    wob = nc.dram_tensor("wob", [E, E], BF16, kind="Internal").ap()

    P = Prog()

    with contextlib.ExitStack() as es:
        AB = es.enter_context(nc.sbuf_tensor("AB", [128, 62 * 1024], BF16))
        AF32 = es.enter_context(nc.sbuf_tensor("AF32", [128, 15 * 1024], F32))
        pb = [es.enter_context(nc.psum_tensor(f"pb{i}", [128, 512], F32))
              for i in range(8)]
        sems = {}
        for e in ENGINES:
            sems[(e, "eng")] = es.enter_context(nc.semaphore(f"s_{e}"))
        for e in ("sp", "gp"):
            sems[(e, "dma")] = es.enter_context(nc.semaphore(f"q_{e}"))

        aboff = [0]
        afoff = [0]

        def abal(n):
            o = aboff[0]; aboff[0] += n
            assert aboff[0] <= AB.shape[1], f"AB overflow {aboff[0]}"
            return o

        def afal(n):
            o = afoff[0]; afoff[0] += n
            assert afoff[0] <= AF32.shape[1], f"AF32 overflow {afoff[0]}"
            return o

        o = abal(16 * 128); indT = AB[0:16, o:o + 16 * 128]
        o = abal(2 * E); brow_b = AB[0:1, o:o + 2 * E]
        bvh = brow_b[0:1, 0:E]; bvl = brow_b[0:1, E:2 * E]
        o = abal(130); ones_r = AB[0:1, o:o + 128]
        o = abal(512); g16 = AB[0:16, o:o + 512]
        o = abal(4 * 512); gbh4 = [AB[:, o + j * 512:o + (j + 1) * 512] for j in range(4)]
        AB_ST = aboff[0]

        o = afal(128); identT = AF32[:, o:o + 128]
        o = afal(nDELTA); iota28T = AF32[:, o:o + nDELTA]
        o = afal(2); hm2_sb = AF32[:, o:o + 2]
        o = afal(128); hm2T_sb = AF32[0:2, o:o + 128]
        o = afal(128); onesf_sb = AF32[0:1, o:o + 128]
        o = afal(3); constf = AF32[:, o:o + 3]
        o = afal(8 * NBLK); routsb = AF32[:, o:o + 8 * NBLK]
        o = afal(8 * H); wdsb = AF32[:, o:o + 8 * H]
        o = afal(4 * nQ)
        bqE_sb = AF32[:, o:o + nQ]; bqO_sb = AF32[:, o + nQ:o + 2 * nQ]
        bkE_sb = AF32[:, o + 2 * nQ:o + 3 * nQ]; bkO_sb = AF32[:, o + 3 * nQ:o + 4 * nQ]
        o = afal(8 + HID // 128 + 16)
        bg_sb = AF32[:, o:o + 8]; b1_sb = AF32[:, o + 8:o + 8 + HID // 128]
        gam_sb = AF32[:, o + 8 + HID // 128:o + 16 + HID // 128]
        bet_sb = AF32[:, o + 16 + HID // 128:o + 24 + HID // 128]
        o = afal(4 * E); brow_f = AF32[0:1, o:o + 4 * E]
        bv_sb = brow_f[0:1, 0:E]; bo_sb = brow_f[0:1, E:2 * E]
        b2_sb = brow_f[0:1, 2 * E:3 * E]; bob2_sb = brow_f[0:1, 3 * E:4 * E]
        o = afal(32); rowsm_sb = AF32[0:1, o:o + 32]
        rb_sb = rowsm_sb[0:1, 0:NBLK]; bd_sb = rowsm_sb[0:1, NBLK:NBLK + H]
        o = afal(1); bdc_sb = AF32[0:H, o:o + 1]
        o = afal(1); rbc_sb = AF32[0:NBLK, o:o + 1]
        o = afal(512); scr = AF32[:, o:o + 512]
        o = afal(64); stat = AF32[:, o:o + 64]
        o = afal(H); ldrow = AF32[0:1, o:o + H]
        o = afal(1); ldcol = AF32[0:H, o:o + 1]
        o = afal(H); ldbc = AF32[:, o:o + H]
        o = afal(H); ld8bc = AF32[:, o:o + H]
        o = afal(H * nDELTA); hibias = AF32[:, o:o + H * nDELTA]
        o = afal(8); xmean = AF32[:, o:o + 8]
        AF_ST = afoff[0]

        o = abal(2 * 8 * 512)
        xnt = [AB[:, o:o + 4096], AB[:, o + 4096:o + 8192]]
        AB_C0 = aboff[0]
        AF_D0 = afoff[0]

        aboff[0] = AB_C0; afoff[0] = AF_D0
        o = abal(8 * 512); xstg = AB[:, o:o + 4096]
        o = abal(512); trash_b = AB[:, o:o + 512]
        o = abal(2 * 4096); wstg = [AB[:, o:o + 4096], AB[:, o + 4096:o + 8192]]
        o = abal(T); maskb = AB[0:16, o:o + T]
        p1_ab = aboff[0]
        o = afal(2 * E); xin = [AF32[:, o:o + E], AF32[:, o + E:o + 2 * E]]
        o = afal(2 * E); xn32 = [AF32[:, o:o + E], AF32[:, o + E:o + 2 * E]]
        o = afal(8 * 512); ring = AF32[:, o:o + 4096]
        o = afal(512); ltile = AF32[0:16, o:o + 512]
        o = afal(64); ltm = AF32[:, o:o + 64]
        o = afal(192); mbtm = AF32[:, o:o + 192]
        o = afal(4); mxs = AF32[:, o:o + 4]
        o = afal(64); xpart = AF32[:, o:o + 64]
        iotaq_sb = ring[0:16, 0:512]
        gsc = ring[0:16, 512:1024]
        p1_af = afoff[0]

        aboff[0] = AB_C0; afoff[0] = AF_D0
        o = abal(2048)
        ropes = [AB[:, o:o + 2048], AB[:, o:o + 2048]]
        o = abal(4 * 2048)
        wq4 = AB[:, o:o + 2048]; wk4 = AB[:, o + 2048:o + 4096]
        wv4 = AB[:, o + 4096:o + 6144]; wg4 = AB[:, o + 6144:o + 8192]
        o = abal(4 * T)
        QTE = AB[:, o:o + T]; QTO = AB[:, o + T:o + 2 * T]
        KTE = AB[:, o + 2 * T:o + 3 * T]; KTO = AB[:, o + 3 * T:o + 4 * T]
        o = abal(nTC * 260)
        vtm = [AB[:, o + i * 260:o + (i + 1) * 260] for i in range(nTC)]
        o = abal(2 * T)
        ognq = [AB[:, o:o + T], AB[:, o + T:o + 2 * T]]
        o = abal(8 * 512)
        ptile = [AB[:, o + i * 512:o + (i + 1) * 512] for i in range(8)]
        o = abal(512); gate_t = AB[:, o:o + 512]
        o = abal(512); go_t = AB[:, o:o + 512]
        p2_ab = aboff[0]
        o = afal(4 * 512); rtmp = [AF32[:, o + i * 512:o + (i + 1) * 512] for i in range(4)]
        o = afal(4 * 512); tlo = [AF32[0:65, o + i * 512:o + (i + 1) * 512] for i in range(4)]
        o = afal(4 * 512); ogu = [AF32[0:65, o + i * 512:o + (i + 1) * 512] for i in range(4)]
        o = afal(64); sstat = AF32[:, o:o + 64]
        o = afal(4 * 512); tdel = [AF32[:, o + i * 512:o + (i + 1) * 512] for i in range(4)]
        p2_af = afoff[0]

        aboff[0] = AB_C0; afoff[0] = AF_D0
        o = abal(32 * 512); hrel = AB[:, o:o + 32 * 512]
        o = abal(2 * 1024); w1s = [AB[:, o:o + 1024], AB[:, o + 1024:o + 2048]]
        o = abal(2 * 1024); w2s = [AB[:, o:o + 1024], AB[:, o + 1024:o + 2048]]
        o = abal(8 * 1024); wos = AB[:, o:o + 8192]
        o = abal(2 * 4096); go3 = [AB[:, o:o + 4096], AB[:, o + 4096:o + 8192]]
        o = abal(NBLK * 512); mrow = AB[0:1, o:o + NBLK * 512]
        o = abal(2 * E); bbh = AB[0:1, o:o + E]; bbl = AB[0:1, o + E:o + 2 * E]
        p3_ab = aboff[0]
        o = afal(2 * E); xres = [AF32[:, o:o + E], AF32[:, o + E:o + 2 * E]]
        o = afal(2 * E); oev = [AF32[:, o:o + E], AF32[:, o + E:o + 2 * E]]
        p3_af = afoff[0]

        assert max(p1_ab, p2_ab, p3_ab) <= AB.shape[1], (p1_ab, p2_ab, p3_ab)
        assert max(p1_af, p2_af, p3_af) <= AF32.shape[1], (p1_af, p2_af, p3_af)

        with nc.Block() as block:
            CL = {}

            def cload(key, dst, src):
                CL[key] = P.dma("sp", dst, src)

            cload("ident", identT, identFd[:, :])
            cload("i28", iota28T, iota28d[:, :])
            cload("hm2", hm2_sb, hm2d[:, :])
            cload("hm2T", hm2T_sb, hm2Td[:, :])
            cload("onesf", onesf_sb, onesfd[:, :])
            cload("constf", constf, constfd[:, :])
            cload("rout", routsb.rearrange("p (c m) -> p c m", c=8),
                  routT.rearrange("(c p) m -> p c m", p=128))
            cload("wd", wdsb.rearrange("p (c m) -> p c m", c=8),
                  wdT.rearrange("(c p) m -> p c m", p=128))
            cload("bqE", bqE_sb, bqEd[:, :])
            cload("bqO", bqO_sb, bqOd[:, :])
            cload("bkE", bkE_sb, bkEd[:, :])
            cload("bkO", bkO_sb, bkOd[:, :])
            cload("bg", bg_sb, bg_cd[:, :])
            cload("b1", b1_sb, b1_cd[:, :])
            cload("gam", gam_sb, gam_cd[:, :])
            cload("bet", bet_sb, bet_cd[:, :])
            cload("brf", brow_f, browsd[:, :])
            cload("rowsm", rowsm_sb, rowsmd[:, :])
            cload("bdc", bdc_sb, bd_cd[:, :])
            cload("rbc", rbc_sb, rb_cd[:, :])
            cload("ind", indT, ind16d[:, :])

            ms1 = P.add("dve", lambda e: e.memset(ones_r, 1.0))
            s0 = P.add("dve", lambda e: e.tensor_tensor(bob2_sb, bo_sb, b2_sb, ALU.add),
                       [CL["brf"]])
            def hilo_split(hi, lo, srcrow, extra_deps):
                splits = []
                c1 = P.add("dve", lambda e, hi=hi, srcrow=srcrow: e.tensor_copy(
                    hi, srcrow), [s0, CL["brf"]] + extra_deps)
                for half in range(2):
                    sl = slice(half * 512, (half + 1) * 512)
                    d1 = P.add("dve", lambda e, hi=hi, srcrow=srcrow, sl=sl: e.tensor_tensor(
                        scr[0:1, 0:512], srcrow[0:1, sl], hi[0:1, sl], ALU.subtract),
                        [c1] + splits[-1:])
                    d2 = P.add("dve", lambda e, lo=lo, sl=sl: e.tensor_copy(
                        lo[0:1, sl], scr[0:1, 0:512]), [d1])
                    splits.append(d2)
                return splits
            bias_ready = hilo_split(bvh, bvl, bv_sb, []) + [ms1]

            precast = []
            prev_st = [None, None]
            k = 0
            for (src, dst, rows, cols) in ((w1T, w1b, E, HID), (w2T, w2b, HID, E),
                                           (woT, wob, E, E)):
                for rc in range(rows // 128):
                    bi = k % 2
                    ld = P.dma("gp", wstg[bi][:, 0:cols], src[rc * 128:(rc + 1) * 128, :],
                               [prev_st[bi]])
                    st = P.dma("sp", dst[rc * 128:(rc + 1) * 128, :], wstg[bi][:, 0:cols],
                               [ld])
                    prev_st[bi] = st
                    precast.append(st)
                    k += 1

            all_out_stores = []
            batch_tail = None

            for b in range(P2C):
                xn_stores = []
                logit_stores = []
                xpart_ops = []
                prev_xin = [batch_tail, batch_tail]
                prev_ring = [batch_tail] if batch_tail else []
                prev_ltile = batch_tail
                tr_bank_free = [batch_tail, batch_tail]
                xstg_guard = [batch_tail] * 8
                for tb in range(nTB):
                    ring_writers = []
                    for tci in range(4):
                        tc = tb * 4 + tci
                        bi = tc % 2
                        SO = 16 * bi
                        ld = P.dma("sp", xin[bi], x_in[b, tc * 128:(tc + 1) * 128, :],
                                   [prev_xin[bi]])
                        r1 = P.add("dve", lambda e, bi=bi, SO=SO: e.reduce_sum(
                            stat[:, SO + 0:SO + 1], xin[bi], axis=AX.X), [ld])
                        r2 = P.add("act", lambda e, bi=bi, SO=SO: e.activation(
                            trash_b[:, 0:512], xin[bi][:, 0:512], AF.Square,
                            accum_out=stat[:, SO + 1:SO + 2]), [ld])
                        r2b = P.add("act", lambda e, bi=bi, SO=SO: e.activation(
                            trash_b[:, 0:512], xin[bi][:, 512:1024], AF.Square,
                            accum_out=stat[:, SO + 2:SO + 3]), [r2])
                        t0 = P.add("dve", lambda e, SO=SO: e.tensor_tensor(
                            stat[:, SO + 3:SO + 4], stat[:, SO + 1:SO + 2], stat[:, SO + 2:SO + 3], ALU.add), [r2b])
                        t1 = P.add("dve", lambda e, SO=SO: e.tensor_single_scalar(
                            stat[:, SO + 4:SO + 5], stat[:, SO + 0:SO + 1], 1.0 / E, ALU.mult), [r1])
                        t2 = P.add("dve", lambda e, SO=SO: e.tensor_tensor(
                            stat[:, SO + 5:SO + 6], stat[:, SO + 4:SO + 5], stat[:, SO + 4:SO + 5], ALU.mult), [t1])
                        t3 = P.add("dve", lambda e, SO=SO: e.scalar_tensor_tensor(
                            stat[:, SO + 6:SO + 7], stat[:, SO + 3:SO + 4], 1.0 / E, stat[:, SO + 5:SO + 6],
                            ALU.mult, ALU.subtract), [t0, t2])
                        a1 = P.add("act", lambda e, SO=SO: e.activation(
                            stat[:, SO + 7:SO + 8], stat[:, SO + 6:SO + 7], AF.Ln,
                            bias=np.float32(1e-5)), [t3])
                        a2 = P.add("act", lambda e, SO=SO: e.activation(
                            stat[:, SO + 8:SO + 9], stat[:, SO + 7:SO + 8], AF.Exp, scale=-0.5), [a1])
                        t4 = P.add("dve", lambda e, SO=SO: e.tensor_tensor(
                            stat[:, SO + 9:SO + 10], stat[:, SO + 4:SO + 5], stat[:, SO + 8:SO + 9], ALU.mult), [a2])
                        t5 = P.add("dve", lambda e, SO=SO: e.tensor_single_scalar(
                            stat[:, SO + 10:SO + 11], stat[:, SO + 9:SO + 10], -1.0, ALU.mult), [t4])
                        def _lnapply(e, bi=bi, SO=SO):
                            ob, bb = bass.broadcast_tensor_aps(xn32[bi], stat[:, SO + 10:SO + 11])
                            return e.scalar_tensor_tensor(
                                xn32[bi], xin[bi], stat[:, SO + 8:SO + 9], bb,
                                ALU.mult, ALU.add)
                        ap = P.add("dve", _lnapply, [t5])
                        prev_xin[bi] = ap
                        for ec in range(8):
                            bank = pb[ec % 2]
                            tr = P.add("pe", lambda e, bi=bi, ec=ec, bank=bank: e.transpose(
                                bank[:, 0:128], xn32[bi][:, ec * 128:(ec + 1) * 128],
                                identT),
                                [ap, CL["ident"], tr_bank_free[ec % 2]]
                                + (prev_ring if (tci == 0 and ec < 2) else []))
                            cp = P.add("dve", lambda e, ec=ec, tci=tci, bank=bank: e.tensor_copy(
                                ring[:, ec * 512 + tci * 128:ec * 512 + tci * 128 + 128],
                                bank[:, 0:128]), [tr])
                            tr_bank_free[ec % 2] = cp
                            ring_writers.append(cp)
                    ring_readers = []
                    last_rmm = None
                    for ec in range(8):
                        deps = ring_writers[ec::8]
                        cst = P.add("dve", lambda e, ec=ec: e.tensor_copy(
                            xstg[:, ec * 512:(ec + 1) * 512],
                            ring[:, ec * 512:(ec + 1) * 512]),
                            deps + [xstg_guard[ec]])
                        st = P.dma("sp", xnTs[b].rearrange("(c p) t -> p c t", p=128)[
                            :, ec, tb * 512:(tb + 1) * 512],
                            xstg[:, ec * 512:(ec + 1) * 512], [cst])
                        xn_stores.append(st)
                        xstg_guard[ec] = st
                        xp = P.add("dve", lambda e, ec=ec, tb=tb: e.reduce_sum(
                            xpart[:, ec * 8 + tb:ec * 8 + tb + 1],
                            ring[:, ec * 512:(ec + 1) * 512], axis=AX.X), deps)
                        xpart_ops.append(xp)
                        mm = P.add("pe", lambda e, ec=ec: e.matmul(
                            pb[2][0:16, 0:512], routsb[:, ec * 16:(ec + 1) * 16],
                            ring[:, ec * 512:(ec + 1) * 512],
                            start=(ec == 0), stop=(ec == 7)),
                            deps + [CL["rout"], prev_ltile])
                        last_rmm = mm
                        ring_readers.extend([st, xp, mm])
                    le = P.add("dve", lambda e: e.tensor_single_scalar(
                        ltile, pb[2][0:16, 0:512], rbc_sb, ALU.add),
                        [last_rmm, prev_ltile, CL["rbc"]])
                    lst = P.dma("sp", logitD[b][:, tb * 512:(tb + 1) * 512], ltile, [le])
                    logit_stores.append(lst)
                    prev_ltile = lst
                    prev_ring = ring_readers + [le]

                xm_ops = []
                for ec in range(8):
                    m1 = P.add("dve", lambda e, ec=ec: e.reduce_sum(
                        stat[:, 11:12], xpart[:, ec * 8:ec * 8 + nTB], axis=AX.X),
                        xpart_ops[ec::8])
                    m2 = P.add("dve", lambda e, ec=ec: e.tensor_single_scalar(
                        xmean[:, ec:ec + 1], stat[:, 11:12], 1.0 / T, ALU.mult), [m1])
                    xm_ops.append(m2)
                pdr = None
                for ec in range(8):
                    pdr = P.add("pe", lambda e, ec=ec: e.matmul(
                        pb[3][0:1, 0:H], xmean[:, ec:ec + 1], wdsb[:, ec * H:(ec + 1) * H],
                        start=(ec == 0), stop=(ec == 7)), [xm_ops[ec], CL["wd"]])
                pdc = None
                for ec in range(8):
                    pdc = P.add("pe", lambda e, ec=ec: e.matmul(
                        pb[4][0:H, 0:1], wdsb[:, ec * H:(ec + 1) * H], xmean[:, ec:ec + 1],
                        start=(ec == 0), stop=(ec == 7)), [xm_ops[ec], CL["wd"]])
                r0 = P.add("dve", lambda e: e.tensor_tensor(
                    scr[0:1, 0:H], pb[3][0:1, 0:H], bd_sb, ALU.add), [pdr, CL["rowsm"]])
                r1 = P.add("act", lambda e: e.activation(
                    scr[0:1, 16:32], scr[0:1, 0:16], AF.Sigmoid), [r0])
                r2 = P.add("act", lambda e: e.activation(
                    scr[0:1, 32:48], scr[0:1, 16:32], AF.Exp,
                    scale=-8.0 * LN2, bias=np.float32(-5.0 * LN2)), [r1])
                r3 = P.add("dve", lambda e: e.tensor_scalar(
                    scr[0:1, 48:64], scr[0:1, 32:48], -1.0, 1.0, ALU.mult, ALU.add),
                    [r2])
                r4 = P.add("act", lambda e: e.activation(
                    ldrow, scr[0:1, 48:64], AF.Ln), [r3])
                c0 = P.add("dve", lambda e: e.tensor_tensor(
                    stat[0:H, 12:13], pb[4][0:H, 0:1], bdc_sb, ALU.add),
                    [pdc, CL["bdc"]])
                c1 = P.add("act", lambda e: e.activation(
                    stat[0:H, 13:14], stat[0:H, 12:13], AF.Sigmoid), [c0])
                c2 = P.add("act", lambda e: e.activation(
                    stat[0:H, 14:15], stat[0:H, 13:14], AF.Exp,
                    scale=-8.0 * LN2, bias=np.float32(-5.0 * LN2)), [c1])
                c3 = P.add("dve", lambda e: e.tensor_scalar(
                    stat[0:H, 15:16], stat[0:H, 14:15], -1.0, 1.0, ALU.mult, ALU.add),
                    [c2])
                c4 = P.add("act", lambda e: e.activation(
                    ldcol, stat[0:H, 15:16], AF.Ln), [c3])
                mm1 = P.add("pe", lambda e: e.matmul(
                    pb[3][:, 0:H], onesf_sb, ldrow, start=True, stop=True),
                    [r4, CL["onesf"], r0])
                cp1 = P.add("dve", lambda e: e.tensor_copy(ldbc, pb[3][:, 0:H]), [mm1])
                cp2 = P.add("dve", lambda e: e.tensor_single_scalar(
                    ld8bc, ldbc, 8.0, ALU.mult), [cp1])
                hib_ops = []
                for h in range(H):
                    hib_ops.append(P.add("dve", lambda e, h=h: e.tensor_scalar(
                        hibias[:, h * nDELTA:(h + 1) * nDELTA], iota28T,
                        ldbc[:, h:h + 1], -C_SHIFT, ALU.mult, ALU.add),
                        [cp1, CL["i28"]]))
                ldq = P.dma("sp", iotaq_sb, iotaqd[:, :],
                            [xn_stores[-1], logit_stores[-1]])
                g0 = P.add("dve", lambda e: e.tensor_single_scalar(
                    gsc, iotaq_sb, ldcol, ALU.mult), [c4, ldq])
                g1 = P.add("act", lambda e: e.activation(g16, gsc, AF.Exp), [g0])

                mask_ops = []
                prev_tr = c0
                prev_lload = None
                for grp in range(nTC // 4):
                    ll = P.dma("sp", ltile, logitD[b][:, grp * 512:(grp + 1) * 512],
                               [logit_stores[grp], prev_lload])
                    tr_cp = []
                    for i in range(4):
                        tr = P.add("pe", lambda e, i=i: e.transpose(
                            pb[5][:, 0:16], ltile[0:16, i * 128:(i + 1) * 128],
                            identT[0:16, 0:16]), [ll, prev_tr])
                        cp = P.add("dve", lambda e, i=i: e.tensor_copy(
                            ltm[:, i * 16:(i + 1) * 16], pb[5][:, 0:16]), [tr])
                        prev_tr = cp
                        tr_cp.append(cp)
                    cur = ltm[:, 0:64].rearrange("p (i j) -> p i j", j=16)
                    ge = mbtm[:, 0:64].rearrange("p (i j) -> p i j", j=16)
                    prev = tr_cp
                    for it in range(4):
                        m1 = P.add("dve", lambda e: e.tensor_reduce(
                            mxs[:, 0:4], cur, axis=AX.X, op=ALU.max), prev)
                        mxb = mxs[:, 0:4].rearrange("p (i j) -> p i j", j=1)
                        _, mxbb = bass.broadcast_tensor_aps(cur, mxb)
                        g2 = P.add("dve", lambda e, mxbb=mxbb: e.tensor_tensor(
                            ge, cur, mxbb, ALU.is_ge), [m1])
                        u1 = P.add("dve", lambda e: e.scalar_tensor_tensor(
                            cur, ge, -1e30, cur, ALU.mult, ALU.add), [g2])
                        prev = [u1]
                    fm = P.add("dve", lambda e: e.tensor_single_scalar(
                        mbtm[:, 64:128], ltm[:, 0:64], -1e29, ALU.is_le), prev)
                    mb = P.add("dve", lambda e: e.tensor_scalar(
                        mbtm[:, 128:192], mbtm[:, 64:128], BIG, -BIG, ALU.mult,
                        ALU.add), [fm])
                    for i in range(4):
                        tc = grp * 4 + i
                        tr = P.add("pe", lambda e, i=i: e.transpose(
                            pb[5][0:16, 0:128], mbtm[:, 128 + i * 16:128 + (i + 1) * 16],
                            identT), [mb, prev_tr])
                        cp = P.add("dve", lambda e, tc=tc: e.tensor_copy(
                            maskb[0:16, tc * 128:(tc + 1) * 128], pb[5][0:16, 0:128]),
                            [tr])
                        prev_tr = cp
                        mask_ops.append(cp)
                    prev_lload = tr_cp[-1]
                mst = P.dma("sp", maskD[b], maskb, mask_ops)
                p15_tail = mask_ops[-1]
                tables_done = hib_ops + [cp2, g1, p15_tail]

                go_stores = []
                td_loads = []
                for d in range(4):
                    td_loads.append(P.dma("sp", tdel[d], tdeltad[:, d * 512:(d + 1) * 512],
                                          [p15_tail, mst]))
                quad_tail = None
                prev_wl = None
                for q in range(nQ):
                    wdeps = (list(xn_stores) + [p15_tail, mst]) if q == 0 else [quad_tail, prev_wl]
                    wl = []
                    for (wsb, wdram) in ((wq4, wqT), (wk4, wkT), (wv4, wvT), (wg4, wgT)):
                        wl.append(P.dma("gp",
                                        wsb.rearrange("p (c m) -> p c m", c=8),
                                        wdram.rearrange("(c p) m -> p c m", p=128)[
                                            :, :, q * 256:(q + 1) * 256], wdeps))
                    prev_wl = wl[3]

                    rope_done = []
                    v_done = []
                    rope_guard = [quad_tail]
                    prev_xl = [quad_tail, quad_tail]
                    bank01_free = quad_tail
                    rtmp_guard = [quad_tail] * 4
                    for tb in range(nTB):
                        bi = tb % 2
                        xdeps = [prev_xl[bi]] + (list(xn_stores) if (q == 0 and tb < 2) else [])
                        xl = P.dma("sp",
                                   xnt[bi].rearrange("p (c t) -> p c t", c=8),
                                   xnTs[b].rearrange("(c p) t -> p c t", p=128)[
                                       :, :, tb * 512:(tb + 1) * 512], xdeps)
                        rl = P.dma("sp",
                                   ropes[bi].rearrange("p (f t) -> p f t", f=4),
                                   ropetabd.rearrange("p (f t) -> p f t", f=4)[
                                       :, :, tb * 512:(tb + 1) * 512],
                                   [prev_xl[bi], rope_guard[0]])
                        cosE_s = ropes[bi][:, 0:512]; nsinE_s = ropes[bi][:, 512:1024]
                        cosO_s = ropes[bi][:, 1024:1536]; sinO_s = ropes[bi][:, 1536:2048]
                        tb_users = []
                        sl = slice(tb * 512, (tb + 1) * 512)
                        for (wsb, wdep, bEc, bOc, dstE, dstO) in (
                                (wq4, wl[0], bqE_sb, bqO_sb, QTE, QTO),
                                (wk4, wl[1], bkE_sb, bkO_sb, KTE, KTO)):
                            mmE = None
                            for ec in range(8):
                                mmE = P.add("pe", lambda e, ec=ec, wsb=wsb, bi=bi: e.matmul(
                                    pb[0][:, 0:512], wsb[:, ec * 256:ec * 256 + 128],
                                    xnt[bi][:, ec * 512:(ec + 1) * 512],
                                    start=(ec == 0), stop=(ec == 7)),
                                    [xl, wdep] + ([bank01_free] if ec == 0 else []))
                            mmO = None
                            for ec in range(8):
                                mmO = P.add("pe", lambda e, ec=ec, wsb=wsb, bi=bi: e.matmul(
                                    pb[1][:, 0:512], wsb[:, ec * 256 + 128:ec * 256 + 256],
                                    xnt[bi][:, ec * 512:(ec + 1) * 512],
                                    start=(ec == 0), stop=(ec == 7)),
                                    [xl, wdep] + ([bank01_free] if ec == 0 else []))
                            e1 = P.add("dve", lambda e, bEc=bEc, q=q, cosE_s=cosE_s: e.scalar_tensor_tensor(
                                rtmp[0], pb[0][:, 0:512], bEc[:, q:q + 1], cosE_s,
                                ALU.add, ALU.mult),
                                [mmE, rl, CL["bqE"], CL["bkE"], rtmp_guard[0]])
                            e2 = P.add("dve", lambda e, bOc=bOc, q=q, nsinE_s=nsinE_s: e.scalar_tensor_tensor(
                                rtmp[1], pb[1][:, 0:512], bOc[:, q:q + 1], nsinE_s,
                                ALU.add, ALU.mult),
                                [mmO, rl, CL["bqO"], CL["bkO"], rtmp_guard[1]])
                            e3 = P.add("dve", lambda e, dstE=dstE, sl=sl: e.tensor_tensor(
                                dstE[:, sl], rtmp[0], rtmp[1], ALU.add), [e1, e2])
                            e4 = P.add("dve", lambda e, bOc=bOc, q=q, cosO_s=cosO_s: e.scalar_tensor_tensor(
                                rtmp[2], pb[1][:, 0:512], bOc[:, q:q + 1], cosO_s,
                                ALU.add, ALU.mult), [mmO, rtmp_guard[2]])
                            e5 = P.add("dve", lambda e, bEc=bEc, q=q, sinO_s=sinO_s: e.scalar_tensor_tensor(
                                rtmp[3], pb[0][:, 0:512], bEc[:, q:q + 1], sinO_s,
                                ALU.add, ALU.mult), [mmE, rtmp_guard[3]])
                            e6 = P.add("dve", lambda e, dstO=dstO, sl=sl: e.tensor_tensor(
                                dstO[:, sl], rtmp[2], rtmp[3], ALU.add), [e4, e5])
                            rope_done.extend([e3, e6])
                            tb_users.extend([e3, e6])
                            bank01_free = e6
                            rope_guard[0] = e6
                        prev_vstep = None
                        for tci in range(4):
                            tc = tb * 4 + tci
                            mmv = None
                            for ec in range(8):
                                mmv = P.add("pe", lambda e, ec=ec, tci=tci, bi=bi: e.matmul(
                                    pb[2][:, 0:256],
                                    xnt[bi][:, ec * 512 + tci * 128:ec * 512 + tci * 128 + 128],
                                    wv4[:, ec * 256:(ec + 1) * 256],
                                    start=(ec == 0), stop=False),
                                    [xl, wl[2]] + ([v_done[-1]] if v_done else [quad_tail]))
                            mh = P.add("pe", lambda e, q=q: e.matmul(
                                pb[2][:, 0:256], ones_r, bvh[0:1, q * 256:(q + 1) * 256],
                                start=False, stop=False), [mmv] + bias_ready)
                            mlo = P.add("pe", lambda e, q=q: e.matmul(
                                pb[2][:, 0:256], ones_r, bvl[0:1, q * 256:(q + 1) * 256],
                                start=False, stop=True), [mh])
                            vc = P.add("dve", lambda e, tc=tc: e.tensor_copy(
                                vtm[tc].rearrange("p (h d) -> p h d", d=65)[:, :, 0:64],
                                pb[2][:, 0:256].rearrange("p (h d) -> p h d", d=64)),
                                [mlo])
                            vo = P.add("dve", lambda e, tc=tc: e.memset(
                                vtm[tc].rearrange("p (h d) -> p h d", d=65)[:, :, 64:65],
                                1.0), [vc])
                            v_done.append(vo)
                            tb_users.append(vo)
                        prev_xl[bi] = tb_users[-1]

                    gb_done = []
                    prev_gb = cp2 if q == 0 else quad_tail
                    for j in range(4):
                        h = 4 * q + j
                        mm = P.add("pe", lambda e, h=h: e.matmul(
                            pb[3][:, 0:512], indT[0:16, h * 128:(h + 1) * 128], g16,
                            start=True, stop=True), [g1, CL["ind"], prev_gb])
                        cp = P.add("dve", lambda e, j=j: e.tensor_copy(
                            gbh4[j], pb[3][:, 0:512]), [mm])
                        prev_gb = cp
                        gb_done.append(cp)

                    qkv_done = rope_done + v_done + tables_done + td_loads + gb_done

                    ogn_written = []
                    stat_ops = []
                    og_free = [quad_tail] * 4
                    s_free = [quad_tail] * 4
                    pt_free = [quad_tail] * 8
                    exp_rtmp_guard = [None] * 4
                    for qb in range(nQB):
                        q0 = qb * 512
                        kd0 = q0 // 128
                        av_last = [None] * 4
                        tlo_ops = [None] * 4

                        def scores(k, extra):
                            res = []
                            for j in range(4):
                                mmE = P.add("pe", lambda e, j=j, k=k, q0=q0: e.matmul(
                                    pb[4 + j][:, 0:512],
                                    KTE[32 * j:32 * j + 32, k * 128:(k + 1) * 128],
                                    QTE[32 * j:32 * j + 32, q0:q0 + 512],
                                    start=True, stop=False, tile_position=(32 * j, 0)),
                                    [s_free[j]] + extra)
                                mmO = P.add("pe", lambda e, j=j, k=k, q0=q0: e.matmul(
                                    pb[4 + j][:, 0:512],
                                    KTO[32 * j:32 * j + 32, k * 128:(k + 1) * 128],
                                    QTO[32 * j:32 * j + 32, q0:q0 + 512],
                                    start=False, stop=True, tile_position=(32 * j, 0)),
                                    [mmE])
                                res.append(mmO)
                            return res

                        def exp_av(k, kind, sc_ops, first_av, last_av):
                            for j in range(4):
                                h = 4 * q + j
                                slot = (k % 2) * 4 + j
                                pt = ptile[slot]
                                ptdep = pt_free[slot]
                                if kind == "diag":
                                    d = k - kd0
                                    stx = P.add("dve", lambda e, j=j, d=d, h=h: e.scalar_tensor_tensor(
                                        rtmp[j], tdel[d], ld8bc[:, h:h + 1],
                                        pb[4 + j][:, 0:512], ALU.mult, ALU.add),
                                        [sc_ops[j], td_loads[d], cp2,
                                         exp_rtmp_guard[j]])
                                    ex = P.add("act", lambda e, pt=pt, j=j: e.activation(
                                        pt, rtmp[j], AF.Exp, scale=0.125,
                                        bias=np.float32(-C_SHIFT)), [stx, ptdep])
                                    exp_rtmp_guard[j] = ex
                                    s_free[j] = stx
                                elif kind == "hi":
                                    d = k - kd0 - 4
                                    ex = P.add("act", lambda e, pt=pt, j=j, h=h, d=d: e.activation(
                                        pt, pb[4 + j][:, 0:512], AF.Exp, scale=0.125,
                                        bias=hibias[:, h * nDELTA + d:h * nDELTA + d + 1]),
                                        [sc_ops[j], ptdep, hib_ops[h]])
                                    s_free[j] = ex
                                else:
                                    ex = P.add("act", lambda e, pt=pt, j=j: e.activation(
                                        pt, pb[4 + j][:, 0:512], AF.Exp, scale=0.125,
                                        bias=np.float32(-C_SHIFT)), [sc_ops[j], ptdep])
                                    s_free[j] = ex
                                avdeps = [ex, av_last[j]]
                                if first_av:
                                    avdeps.append(og_free[j])
                                mm = P.add("pe", lambda e, j=j, k=k, pt=pt, first_av=first_av, last_av=last_av: e.matmul(
                                    pb[j][0:65, 0:512], vtm[k][:, j * 65:j * 65 + 65],
                                    pt, start=first_av, stop=last_av), avdeps)
                                av_last[j] = mm
                                pt_free[slot] = mm

                        for k in range(0, kd0 + 4):
                            kind = "diag" if k >= kd0 else "lo"
                            sc = scores(k, qkv_done if (qb == 0 and k < 2) else [])
                            exp_av(k, kind, sc, first_av=(k == 0),
                                   last_av=(k == kd0 + 3))
                        for j in range(4):
                            tl = P.add("dve", lambda e, j=j: e.tensor_tensor(
                                tlo[j], pb[j][0:65, 0:512], gbh4[j][0:65, 0:512],
                                ALU.mult), [av_last[j], gb_done[j]])
                            tlo_ops[j] = tl
                            og_free[j] = tl
                        av_last = [None] * 4
                        has_hi = kd0 + 4 < nKC
                        for k in range(kd0 + 4, nKC):
                            sc = scores(k, [])
                            exp_av(k, "hi", sc, first_av=(k == kd0 + 4),
                                   last_av=(k == nKC - 1))
                        for j in range(4):
                            h = 4 * q + j
                            if has_hi:
                                cb = P.add("dve", lambda e, j=j: e.tensor_tensor(
                                    ogu[j], pb[j][0:65, 0:512], tlo[j], ALU.add),
                                    [av_last[j], tlo_ops[j]])
                                og_free[j] = cb
                            else:
                                cb = P.add("dve", lambda e, j=j: e.tensor_copy(
                                    ogu[j], tlo[j]), [tlo_ops[j]])
                            rc = P.add("dve", lambda e, j=j: e.reciprocal(
                                ogu[j][64:65, 0:512], ogu[j][64:65, 0:512]), [cb])
                            rbc = P.add("dve", lambda e, j=j: e.tensor_copy(
                                ptile[j][0:1, 0:512], ogu[j][64:65, 0:512]),
                                [rc, pt_free[j]])
                            zb = P.add("pe", lambda e, j=j: e.matmul(
                                pb[4 + j][:, 0:512], ones_r, ptile[j][0:1, 0:512],
                                start=True, stop=True), [rbc, s_free[j], ms1])
                            pt_free[j] = zb
                            ogn_dst = ognq[j // 2][64 * (j % 2):64 * (j % 2) + 64,
                                                  q0:q0 + 512]
                            nm = P.add("dve", lambda e, j=j, ogn_dst=ogn_dst: e.tensor_tensor(
                                ogn_dst, ogu[j][0:64, 0:512], pb[4 + j][0:64, 0:512],
                                ALU.mult), [zb])
                            s_free[j] = nm
                            ogn_written.append(nm)
                            sp_ = P.add("dve", lambda e, j=j, qb=qb, ogn_dst=ogn_dst: e.reduce_sum(
                                sstat[64 * (j % 2):64 * (j % 2) + 64,
                                      (j // 2) * 16 + qb:(j // 2) * 16 + qb + 1],
                                ogn_dst, axis=AX.X), [nm])
                            sq_ = P.add("act", lambda e, j=j, qb=qb, ogn_dst=ogn_dst: e.activation(
                                gate_t[0:64, 0:512], ogn_dst, AF.Square,
                                accum_out=sstat[64 * (j % 2):64 * (j % 2) + 64,
                                                (j // 2) * 16 + 8 + qb:(j // 2) * 16 + 8 + qb + 1]),
                                [nm])
                            stat_ops.extend([sp_, sq_])

                    prev_xl2 = prev_xl
                    bank3_guard = gb_done[-1]
                    go_tail = None
                    prev_gm = None
                    for pc in range(2):
                        su = P.add("dve", lambda e, pc=pc: e.reduce_sum(
                            sstat[:, 32 + 4 * pc:33 + 4 * pc],
                            sstat[:, pc * 16:pc * 16 + nQB], axis=AX.X), stat_ops)
                        qu = P.add("dve", lambda e, pc=pc: e.reduce_sum(
                            sstat[:, 33 + 4 * pc:34 + 4 * pc],
                            sstat[:, pc * 16 + 8:pc * 16 + 8 + nQB], axis=AX.X),
                            stat_ops)
                        m1 = P.add("pe", lambda e, pc=pc: e.matmul(
                            pb[2][0:2, 0:2], hm2_sb,
                            sstat[:, 32 + 4 * pc:34 + 4 * pc],
                            start=True, stop=True),
                            [su, qu, CL["hm2"], v_done[-1]])
                        t1 = P.add("dve", lambda e: e.tensor_single_scalar(
                            sstat[0:2, 40:42], pb[2][0:2, 0:2], 1.0 / (T * 64.0),
                            ALU.mult), [m1])
                        t2 = P.add("dve", lambda e: e.tensor_tensor(
                            sstat[0:2, 42:43], sstat[0:2, 40:41], sstat[0:2, 40:41],
                            ALU.mult), [t1])
                        t3 = P.add("dve", lambda e: e.tensor_tensor(
                            sstat[0:2, 43:44], sstat[0:2, 41:42], sstat[0:2, 42:43],
                            ALU.subtract), [t2])
                        a1 = P.add("act", lambda e: e.activation(
                            sstat[0:2, 44:45], sstat[0:2, 43:44], AF.Ln,
                            bias=np.float32(1e-5)), [t3])
                        a2 = P.add("act", lambda e: e.activation(
                            sstat[0:2, 45:46], sstat[0:2, 44:45], AF.Exp, scale=-0.5),
                            [a1])
                        m2 = P.add("pe", lambda e: e.matmul(
                            pb[3][:, 0:1], hm2T_sb, sstat[0:2, 40:41],
                            start=True, stop=True), [t1, a2, bank3_guard])
                        m3 = P.add("pe", lambda e: e.matmul(
                            pb[3][:, 1:2], hm2T_sb, sstat[0:2, 45:46],
                            start=False, stop=True, skip_group_check=True), [m2])
                        gidx = q * 2 + pc
                        A1 = P.add("dve", lambda e, gidx=gidx: e.tensor_tensor(
                            sstat[:, 48:49], pb[3][:, 1:2], gam_sb[:, gidx:gidx + 1],
                            ALU.mult), [m3, CL["gam"]])
                        B1 = P.add("dve", lambda e: e.tensor_tensor(
                            sstat[:, 49:50], pb[3][:, 0:1], sstat[:, 48:49], ALU.mult),
                            [A1])
                        B2 = P.add("dve", lambda e, gidx=gidx: e.tensor_tensor(
                            sstat[:, 50:51], bet_sb[:, gidx:gidx + 1], sstat[:, 49:50],
                            ALU.subtract), [B1, CL["bet"]])
                        prev_gate_bank = B1
                        for tb2 in range(nTB):
                            bi2 = tb2 % 2
                            xl2 = P.dma("sp",
                                        xnt[bi2].rearrange("p (c t) -> p c t", c=8),
                                        xnTs[b].rearrange("(c p) t -> p c t", p=128)[
                                            :, :, tb2 * 512:(tb2 + 1) * 512],
                                        [prev_xl2[bi2]])
                            mmg = None
                            for ec in range(8):
                                mmg = P.add("pe", lambda e, ec=ec, pc=pc, bi2=bi2: e.matmul(
                                    pb[3][:, 0:512],
                                    wg4[:, ec * 256 + pc * 128:ec * 256 + pc * 128 + 128],
                                    xnt[bi2][:, ec * 512:(ec + 1) * 512],
                                    start=(ec == 0), stop=(ec == 7)),
                                    [xl2, wl[3]] + ([prev_gate_bank] if ec == 0 else []))
                            prev_xl2[bi2] = mmg
                            ga = P.add("act", lambda e, gidx=gidx: e.activation(
                                gate_t, pb[3][:, 0:512], AF.Silu,
                                bias=bg_sb[:, gidx:gidx + 1]),
                                [mmg, CL["bg"], prev_gm])
                            prev_gate_bank = ga
                            sl2 = slice(tb2 * 512, (tb2 + 1) * 512)
                            def _affine(e, pc=pc, sl2=sl2):
                                ob, bb = bass.broadcast_tensor_aps(
                                    go_t, sstat[:, 50:51])
                                return e.scalar_tensor_tensor(
                                    go_t, ognq[pc][:, sl2], sstat[:, 48:49], bb,
                                    ALU.mult, ALU.add)
                            af1 = P.add("dve", _affine,
                                        [B2, prev_gm] + (ogn_written if tb2 == 0 else []))
                            gm = P.add("dve", lambda e: e.tensor_tensor(
                                go_t, go_t, gate_t, ALU.mult), [af1, ga, prev_gm])
                            st = P.dma("sp", goTs[b].rearrange("(c p) t -> p c t", p=128)[
                                :, q * 2 + pc, tb2 * 512:(tb2 + 1) * 512], go_t, [gm])
                            prev_gm = st
                            go_stores.append(st)
                        bank3_guard = prev_gate_bank
                        go_tail = prev_gm
                    quad_tail = go_tail

                wol = P.dma("sp", wos.rearrange("p (c m) -> p c m", c=8),
                            wob.rearrange("(c p) m -> p c m", p=128),
                            precast + [quad_tail])
                bias3_ready = hilo_split(bbh, bbl, bob2_sb, [quad_tail])
                w1_prev = [quad_tail, quad_tail]
                w2_prev = [quad_tail, quad_tail]
                go_prev = [quad_tail, quad_tail]
                xn_prev = [quad_tail, quad_tail]
                xr_prev = [None, None]
                obank_free = [quad_tail] * 8
                mrow_guard = quad_tail
                out_stores = []
                for tb in range(nTB):
                    bi = tb % 2
                    xl = P.dma("sp",
                               xnt[bi].rearrange("p (c t) -> p c t", c=8),
                               xnTs[b].rearrange("(c p) t -> p c t", p=128)[
                                   :, :, tb * 512:(tb + 1) * 512], [xn_prev[bi]])
                    gl = P.dma("sp",
                               go3[bi].rearrange("p (c t) -> p c t", c=8),
                               goTs[b].rearrange("(c p) t -> p c t", p=128)[
                                   :, :, tb * 512:(tb + 1) * 512],
                               [go_prev[bi]] + (go_stores if tb < 2 else []))
                    ml = P.dma("sp",
                               mrow.rearrange("o (n t) -> o n t", n=NBLK),
                               maskD[b:b + 1, :, tb * 512:(tb + 1) * 512],
                               [mst, mrow_guard])
                    h_done = []
                    for hc in range(HID // 128):
                        wbi = hc % 2
                        wld = P.dma("sp",
                                    w1s[wbi].rearrange("p (c m) -> p c m", c=8),
                                    w1b.rearrange("(c p) m -> p c m", p=128)[
                                        :, :, hc * 128:(hc + 1) * 128],
                                    [w1_prev[wbi]] + (precast if (tb == 0 and hc < 2) else []))
                        mm = None
                        for ec in range(8):
                            mm = P.add("pe", lambda e, ec=ec, wbi=wbi, bi=bi: e.matmul(
                                pb[0][:, 0:512], w1s[wbi][:, ec * 128:(ec + 1) * 128],
                                xnt[bi][:, ec * 512:(ec + 1) * 512],
                                start=(ec == 0), stop=False),
                                [wld, xl] + ([h_done[-1] if h_done else obank_free[7]]
                                             if ec == 0 else []))
                        mk = P.add("pe", lambda e, hc=hc: e.matmul(
                            pb[0][:, 0:512], ones_r,
                            mrow[0:1, (hc // 2) * 512:(hc // 2) * 512 + 512],
                            start=False, stop=True), [mm, ml, ms1])
                        hv = P.add("dve", lambda e, hc=hc: e.tensor_scalar(
                            hrel[:, hc * 512:(hc + 1) * 512], pb[0][:, 0:512],
                            b1_sb[:, hc:hc + 1], 0.0, ALU.add, ALU.max),
                            [mk, CL["b1"]])
                        h_done.append(hv)
                        w1_prev[wbi] = hv
                    mrow_guard = mk
                    obank = [pb[i] for i in (1, 2, 3, 4, 5, 6, 7, 0)]
                    mm_last = [None] * 8
                    for hc in range(HID // 128):
                        wbi = hc % 2
                        w2l = P.dma("sp", w2s[wbi], w2b[hc * 128:(hc + 1) * 128, :],
                                    [w2_prev[wbi]] + (precast if (tb == 0 and hc < 2) else []))
                        for ti in range(8):
                            tci, eh = ti // 2, ti % 2
                            deps = [w2l, h_done[hc], mm_last[ti]]
                            if hc == 0:
                                deps.append(obank_free[ti])
                                if ti == 7:
                                    deps.append(h_done[-1])
                            mm_last[ti] = P.add("pe", lambda e, hc=hc, tci=tci, eh=eh, ti=ti, wbi=wbi: e.matmul(
                                obank[ti][:, 0:512],
                                hrel[:, hc * 512 + tci * 128:hc * 512 + tci * 128 + 128],
                                w2s[wbi][:, eh * 512:(eh + 1) * 512],
                                start=(hc == 0), stop=False), deps)
                        w2_prev[wbi] = mm_last[7]
                    for ti in range(8):
                        tci, eh = ti // 2, ti % 2
                        for ec in range(8):
                            mm_last[ti] = P.add("pe", lambda e, ec=ec, tci=tci, eh=eh, ti=ti, bi=bi: e.matmul(
                                obank[ti][:, 0:512],
                                go3[bi][:, ec * 512 + tci * 128:ec * 512 + tci * 128 + 128],
                                wos[:, ec * 1024 + eh * 512:ec * 1024 + (eh + 1) * 512],
                                start=False, stop=False), [gl, wol, mm_last[ti]])
                        bh = P.add("pe", lambda e, eh=eh, ti=ti: e.matmul(
                            obank[ti][:, 0:512], ones_r, bbh[0:1, eh * 512:(eh + 1) * 512],
                            start=False, stop=False), [mm_last[ti]] + bias3_ready)
                        bl = P.add("pe", lambda e, eh=eh, ti=ti: e.matmul(
                            obank[ti][:, 0:512], ones_r, bbl[0:1, eh * 512:(eh + 1) * 512],
                            start=False, stop=True), [bh])
                        mm_last[ti] = bl
                    for tci in range(4):
                        tc = tb * 4 + tci
                        xr = P.dma("sp", xres[tci % 2],
                                   x_in[b, tc * 128:(tc + 1) * 128, :],
                                   [xr_prev[tci % 2]])
                        evs = []
                        for eh in range(2):
                            ti = tci * 2 + eh
                            ev = P.add("dve", lambda e, tci=tci, eh=eh, ti=ti: e.tensor_tensor(
                                oev[tci % 2][:, eh * 512:(eh + 1) * 512],
                                obank[ti][:, 0:512],
                                xres[tci % 2][:, eh * 512:(eh + 1) * 512], ALU.add),
                                [mm_last[ti], xr])
                            evs.append(ev)
                            obank_free[ti] = ev
                        st = P.dma("sp", out[b, tc * 128:(tc + 1) * 128, :],
                                   oev[tci % 2], evs)
                        xr_prev[tci % 2] = st
                        out_stores.append(st)
                    xn_prev[bi] = h_done[-1]
                    go_prev[bi] = mm_last[7]
                all_out_stores.extend(out_stores)
                batch_tail = out_stores[-1]

            P.add("sp", lambda e: e.nop(), all_out_stores)
            P.emit_into(block, sems, sim_safe=sim_safe, paranoid=paranoid)

    return nc



def _host_prep(inputs, T=4096, ncores=NCORES, p2c=2):
    g = {k: np.asarray(v) for k, v in inputs.items()}
    x = np.ascontiguousarray(g["L0_attn_x"], dtype=np.float32)
    wq = g["L0_attn_wq"]; wk = g["L0_attn_wk"]; wv = g["L0_attn_wv"]
    wg = g["L0_attn_wg"]; wo = g["L0_attn_wo"]; wd = g["L0_attn_wd"]
    bq = g["L0_attn_bq"]; bk = g["L0_attn_bk"]; bv = g["L0_attn_bv"]
    bgv = g["L0_attn_bg"]; bo = g["L0_attn_bo"]; bd = g["L0_attn_bd"]
    gam = g["L0_attn_gamma"]; bet = g["L0_attn_beta"]
    rw = g["L0_ffn_router_w"]; rb = g["L0_ffn_router_b"]
    w1 = g["L0_ffn_w1"]; b1 = g["L0_ffn_b1"]; w2 = g["L0_ffn_w2"]; b2 = g["L0_ffn_b2"]

    def perm_qk(w):
        wt = np.empty((E, E), np.float32)
        for q in range(4):
            for j in range(4):
                h = 4 * q + j
                re = 64 * h + 2 * np.arange(32)
                wt[:, 256 * q + 32 * j:256 * q + 32 * j + 32] = w[re, :].T
                wt[:, 256 * q + 128 + 32 * j:256 * q + 128 + 32 * j + 32] = w[re + 1, :].T
        return np.ascontiguousarray(wt)

    def col_pack(v):
        return np.ascontiguousarray(v.reshape(-1, 128).T)

    def bias_eo(bvec):
        eE = np.zeros((128, 4), np.float32)
        eO = np.zeros((128, 4), np.float32)
        for q in range(4):
            for j in range(4):
                h = 4 * q + j
                eE[32 * j:32 * j + 32, q] = bvec[64 * h + 2 * np.arange(32)]
                eO[32 * j:32 * j + 32, q] = bvec[64 * h + 2 * np.arange(32) + 1]
        return eE, eO

    bqE, bqO = bias_eo(bq)
    bkE, bkO = bias_eo(bk)

    t_ar = np.arange(T, dtype=np.float64)
    invf = 1.0 / (10000.0 ** (np.arange(0, HD, 2, dtype=np.float64) / HD))
    d_e = 2 * np.arange(32)
    th = lambda dims: t_ar[None, :] * invf[(dims % 32)][:, None]
    cosE = np.tile(np.cos(th(d_e)), (4, 1))
    nsinE = np.tile(-np.sin(th(d_e)), (4, 1))
    cosO = np.tile(np.cos(th(d_e + 1)), (4, 1))
    sinO = np.tile(np.sin(th(d_e + 1)), (4, 1))
    ropetab = np.concatenate([cosE, nsinE, cosO, sinO], axis=1).astype(ml_dtypes.bfloat16)

    dj = np.arange(128)[:, None]
    di = np.arange(512)[None, :]
    tdelta = np.concatenate(
        [np.maximum(d * 128 + dj - di, 0).astype(np.float32) for d in range(4)], axis=1)
    iotaq = np.tile(np.arange(512, dtype=np.float32)[None, :], (16, 1))
    nDELTA = max(T // 128 - 4, 1)
    iota28 = (np.tile((512.0 + 128.0 * np.arange(nDELTA, dtype=np.float32))[None, :],
                      (128, 1)) + np.arange(128, dtype=np.float32)[:, None])
    ind16 = np.zeros((16, 16 * 128), np.float32)
    for h in range(16):
        ind16[h, h * 128:(h + 1) * 128] = 1.0
    hm2 = np.zeros((128, 2), np.float32)
    hm2[0:64, 0] = 1.0
    hm2[64:128, 1] = 1.0
    brows = np.zeros((1, 4 * E), np.float32)
    brows[0, 0:E] = bv; brows[0, E:2 * E] = bo; brows[0, 2 * E:3 * E] = b2
    rowsm = np.zeros((1, 32), np.float32)
    rowsm[0, 0:NBLK] = rb; rowsm[0, NBLK:NBLK + H] = bd

    common = dict(
        wqT=perm_qk(wq), wkT=perm_qk(wk),
        wvT=np.ascontiguousarray(wv.T), wgT=np.ascontiguousarray(wg.T),
        woT=np.ascontiguousarray(wo.T),
        w1T=np.ascontiguousarray(w1.T), w2T=np.ascontiguousarray(w2.T),
        routT=np.ascontiguousarray(rw.T), wdT=np.ascontiguousarray(wd.T),
        bqE=bqE, bqO=bqO, bkE=bkE, bkO=bkO,
        brows=brows,
        bgc=col_pack(bgv), b1c=col_pack(b1), gamc=col_pack(gam), betc=col_pack(bet),
        rowsm=rowsm, bdc=np.ascontiguousarray(bd[:, None].astype(np.float32)),
        rbc=np.ascontiguousarray(rb[:, None].astype(np.float32)),
        identF=np.eye(128, dtype=np.float32),
        ropetab=ropetab, tdelta=tdelta, iotaq=iotaq, iota28=iota28,
        ind16=ind16.astype(ml_dtypes.bfloat16), hm2=hm2,
        hm2T=np.ascontiguousarray(hm2.T),
        onesf=np.ones((1, 128), np.float32),
        constf=np.tile(np.array([[1e-5, -5.0 * LN2, -C_SHIFT]], np.float32),
                       (128, 1)),
    )
    nb = x.shape[0]
    in_maps = []
    for c in range(ncores):
        m = dict(common)
        m["x"] = np.ascontiguousarray(x[p2c * c:p2c * (c + 1)])
        in_maps.append(m)
    return in_maps


def kernel(**inputs):
    in_maps = _host_prep(inputs)
    nc = build_program(P2C=2, T=4096, paranoid="pedve")
    res = run_bass_kernel_spmd(nc, in_maps, core_ids=list(range(NCORES)))
    outs = [np.asarray(r["out"]) for r in res.results]
    return np.concatenate(outs, axis=0)


# version 03
# speedup vs baseline: 1.9063x; 1.9063x over previous
import math
import contextlib
import numpy as np
import ml_dtypes
import concourse.bass as bass
import concourse.mybir as mybir
from concourse.bass_utils import run_bass_kernel_spmd

F32 = mybir.dt.float32
BF16 = mybir.dt.bfloat16
AF = mybir.ActivationFunctionType
ALU = mybir.AluOpType
AX = mybir.AxisListType

E = 1024
H = 16
HD = 64
NBLK = 16
HID = 4096
NCORES = 8
C_SHIFT = 12.0
BIG = float(2 ** 30)
LN2 = math.log(2.0)

ENGINES = ("pe", "act", "dve", "gp", "sp")


class Op:
    __slots__ = ("eng", "fn", "deps", "sem_kind", "inc_needed", "cum")

    def __init__(self, eng, fn, deps, sem_kind):
        self.eng = eng
        self.fn = fn
        self.deps = [d for d in deps if d is not None]
        self.sem_kind = sem_kind
        self.inc_needed = False
        self.cum = None


class Prog:
    def __init__(self):
        self.ops = []

    def add(self, eng, fn, deps=()):
        op = Op(eng, fn, deps, "eng")
        self.ops.append(op)
        return op

    def dma(self, queue, out_ap, in_ap, deps=()):
        op = Op(queue, lambda e, o=out_ap, i=in_ap: e.dma_start(out=o, in_=i),
                deps, "dma")
        self.ops.append(op)
        return op

    def emit_into(self, block, sems, sim_safe=False, paranoid=False):
        if sim_safe or paranoid:
            if sim_safe or paranoid is True:
                chain_engs = ENGINES
            elif paranoid in ("soft", "declared"):
                chain_engs = ("act", "dve") if paranoid == "soft" else ()
            else:
                chain_engs = ("act", "dve", "pe")
            pe_lag = 8 if paranoid == "pelag" else 1
            hist = {}
            for op in self.ops:
                key = op.eng
                if key not in chain_engs:
                    continue
                h = hist.setdefault(key, [])
                lag = pe_lag if key == "pe" else 1
                if len(h) >= lag:
                    op.deps.append(h[-lag])
                h.append(op)
                if len(h) > 16:
                    del h[0]
        for op in self.ops:
            for d in op.deps:
                d.inc_needed = True
            if op.sem_kind == "dma":
                op.inc_needed = True
        cums = {}
        for op in self.ops:
            key = (op.eng, op.sem_kind)
            c = cums.get(key, 0)
            if op.inc_needed:
                c += 16 if op.sem_kind == "dma" else 1
            cums[key] = c
            op.cum = c
        per_eng = {e: [] for e in ENGINES}
        for op in self.ops:
            per_eng[op.eng].append(op)

        def runner(eng_name):
            def run(engine):
                waited = {}
                for op in per_eng[eng_name]:
                    for d in op.deps:
                        if (d.eng == eng_name and d.sem_kind == "eng"
                                and not (sim_safe or paranoid)):
                            continue
                        if (paranoid == "soft" and not sim_safe
                                and d.eng == eng_name and d.sem_kind == "eng"
                                and eng_name not in ("act", "dve")):
                            continue
                        if (paranoid == "declared" and not sim_safe
                                and d.eng == eng_name and d.sem_kind == "eng"
                                and eng_name not in ("act", "dve")):
                            continue
                        key = (d.eng, d.sem_kind)
                        if waited.get(key, -1) >= d.cum:
                            continue
                        engine.wait_ge(sems[key], d.cum)
                        waited[key] = d.cum
                    instr = op.fn(engine)
                    if op.inc_needed:
                        assert instr is not None
                        instr.then_inc(sems[(op.eng, op.sem_kind)],
                                       16 if op.sem_kind == "dma" else 1)
                    if sim_safe and op.sem_kind == "dma":
                        engine.wait_ge(sems[(op.eng, "dma")], op.cum)
            return run

        block.tensor(runner("pe"))
        block.scalar(runner("act"))
        block.vector(runner("dve"))
        block.gpsimd(runner("gp"))
        block.sync(runner("sp"))


def build_program(P2C=2, T=4096, dbg=False, sim_safe=False, paranoid=False):
    nQ = H // 4
    nTC = T // 128
    nTB = T // 512
    nKC = T // 128
    nQB = T // 512
    nDELTA = max(nKC - 4, 1)

    nc = bass.Bass("TRN2", target_bir_lowering=False)

    def din(name, shape, dt=F32):
        return nc.dram_tensor(name, shape, dt, kind="ExternalInput").ap()

    x_in = din("x", [P2C, T, E])
    wqT = din("wqT", [E, E]); wkT = din("wkT", [E, E])
    wvT = din("wvT", [E, E]); wgT = din("wgT", [E, E]); woT = din("woT", [E, E])
    w1T = din("w1T", [E, HID]); w2T = din("w2T", [HID, E])
    routT = din("routT", [E, NBLK]); wdT = din("wdT", [E, H])
    bqEd = din("bqE", [128, nQ]); bqOd = din("bqO", [128, nQ])
    bkEd = din("bkE", [128, nQ]); bkOd = din("bkO", [128, nQ])
    browsd = din("brows", [1, 4 * E])
    bg_cd = din("bgc", [128, 8]); b1_cd = din("b1c", [128, HID // 128])
    gam_cd = din("gamc", [128, 8]); bet_cd = din("betc", [128, 8])
    rowsmd = din("rowsm", [1, 32])
    bd_cd = din("bdc", [H, 1]); rb_cd = din("rbc", [NBLK, 1])
    identFd = din("identF", [128, 128])
    ropetabd = din("ropetab", [128, 4 * T], BF16)
    tdeltad = din("tdelta", [128, 4 * 512])
    iotaqd = din("iotaq", [16, 512])
    iota28d = din("iota28", [128, nDELTA])
    ind16d = din("ind16", [16, 16 * 128], BF16)
    hm2d = din("hm2", [128, 2]); hm2Td = din("hm2T", [2, 128])
    onesfd = din("onesf", [1, 128])
    constfd = din("constf", [128, 3])

    out = nc.dram_tensor("out", [P2C, T, E], F32, kind="ExternalOutput").ap()
    ks = "ExternalOutput" if dbg else "Internal"
    xnTs = nc.dram_tensor("xnTs", [P2C, E, T], BF16, kind=ks).ap()
    goTs = nc.dram_tensor("goTs", [P2C, E, T], BF16, kind=ks).ap()
    maskD = nc.dram_tensor("maskD", [P2C, 16, T], BF16, kind=ks).ap()
    logitD = nc.dram_tensor("logitD", [P2C, 16, T], F32, kind=ks).ap()
    w1b = nc.dram_tensor("w1b", [E, HID], BF16, kind="Internal").ap()
    w2b = nc.dram_tensor("w2b", [HID, E], BF16, kind="Internal").ap()
    wob = nc.dram_tensor("wob", [E, E], BF16, kind="Internal").ap()

    P = Prog()

    with contextlib.ExitStack() as es:
        AB = es.enter_context(nc.sbuf_tensor("AB", [128, 62 * 1024], BF16))
        AF32 = es.enter_context(nc.sbuf_tensor("AF32", [128, 15 * 1024], F32))
        pb = [es.enter_context(nc.psum_tensor(f"pb{i}", [128, 512], F32))
              for i in range(8)]
        sems = {}
        for e in ENGINES:
            sems[(e, "eng")] = es.enter_context(nc.semaphore(f"s_{e}"))
        for e in ("sp", "gp"):
            sems[(e, "dma")] = es.enter_context(nc.semaphore(f"q_{e}"))

        aboff = [0]
        afoff = [0]

        def abal(n):
            o = aboff[0]; aboff[0] += n
            assert aboff[0] <= AB.shape[1], f"AB overflow {aboff[0]}"
            return o

        def afal(n):
            o = afoff[0]; afoff[0] += n
            assert afoff[0] <= AF32.shape[1], f"AF32 overflow {afoff[0]}"
            return o

        o = abal(16 * 128); indT = AB[0:16, o:o + 16 * 128]
        o = abal(2 * E); brow_b = AB[0:1, o:o + 2 * E]
        bvh = brow_b[0:1, 0:E]; bvl = brow_b[0:1, E:2 * E]
        o = abal(130); ones_r = AB[0:1, o:o + 128]
        o = abal(512); g16 = AB[0:16, o:o + 512]
        o = abal(4 * 512); gbh4 = [AB[:, o + j * 512:o + (j + 1) * 512] for j in range(4)]
        AB_ST = aboff[0]

        o = afal(128); identT = AF32[:, o:o + 128]
        o = afal(nDELTA); iota28T = AF32[:, o:o + nDELTA]
        o = afal(2); hm2_sb = AF32[:, o:o + 2]
        o = afal(128); hm2T_sb = AF32[0:2, o:o + 128]
        o = afal(128); onesf_sb = AF32[0:1, o:o + 128]
        o = afal(3); constf = AF32[:, o:o + 3]
        o = afal(8 * NBLK); routsb = AF32[:, o:o + 8 * NBLK]
        o = afal(8 * H); wdsb = AF32[:, o:o + 8 * H]
        o = afal(4 * nQ)
        bqE_sb = AF32[:, o:o + nQ]; bqO_sb = AF32[:, o + nQ:o + 2 * nQ]
        bkE_sb = AF32[:, o + 2 * nQ:o + 3 * nQ]; bkO_sb = AF32[:, o + 3 * nQ:o + 4 * nQ]
        o = afal(8 + HID // 128 + 16)
        bg_sb = AF32[:, o:o + 8]; b1_sb = AF32[:, o + 8:o + 8 + HID // 128]
        gam_sb = AF32[:, o + 8 + HID // 128:o + 16 + HID // 128]
        bet_sb = AF32[:, o + 16 + HID // 128:o + 24 + HID // 128]
        o = afal(4 * E); brow_f = AF32[0:1, o:o + 4 * E]
        bv_sb = brow_f[0:1, 0:E]; bo_sb = brow_f[0:1, E:2 * E]
        b2_sb = brow_f[0:1, 2 * E:3 * E]; bob2_sb = brow_f[0:1, 3 * E:4 * E]
        o = afal(32); rowsm_sb = AF32[0:1, o:o + 32]
        rb_sb = rowsm_sb[0:1, 0:NBLK]; bd_sb = rowsm_sb[0:1, NBLK:NBLK + H]
        o = afal(1); bdc_sb = AF32[0:H, o:o + 1]
        o = afal(1); rbc_sb = AF32[0:NBLK, o:o + 1]
        o = afal(512); scr = AF32[:, o:o + 512]
        o = afal(64); stat = AF32[:, o:o + 64]
        o = afal(H); ldrow = AF32[0:1, o:o + H]
        o = afal(1); ldcol = AF32[0:H, o:o + 1]
        o = afal(H); ldbc = AF32[:, o:o + H]
        o = afal(H); ld8bc = AF32[:, o:o + H]
        o = afal(H * nDELTA); hibias = AF32[:, o:o + H * nDELTA]
        o = afal(8); xmean = AF32[:, o:o + 8]
        AF_ST = afoff[0]

        o = abal(2 * 8 * 512)
        xnt = [AB[:, o:o + 4096], AB[:, o + 4096:o + 8192]]
        AB_C0 = aboff[0]
        AF_D0 = afoff[0]

        aboff[0] = AB_C0; afoff[0] = AF_D0
        o = abal(8 * 512); xstg = AB[:, o:o + 4096]
        o = abal(512); trash_b = AB[:, o:o + 512]
        o = abal(2 * 4096); wstg = [AB[:, o:o + 4096], AB[:, o + 4096:o + 8192]]
        o = abal(T); maskb = AB[0:16, o:o + T]
        p1_ab = aboff[0]
        o = afal(2 * E); xin = [AF32[:, o:o + E], AF32[:, o + E:o + 2 * E]]
        o = afal(2 * E); xn32 = [AF32[:, o:o + E], AF32[:, o + E:o + 2 * E]]
        o = afal(8 * 512); ring = AF32[:, o:o + 4096]
        o = afal(512); ltile = AF32[0:16, o:o + 512]
        o = afal(64); ltm = AF32[:, o:o + 64]
        o = afal(192); mbtm = AF32[:, o:o + 192]
        o = afal(4); mxs = AF32[:, o:o + 4]
        o = afal(64); xpart = AF32[:, o:o + 64]
        iotaq_sb = ring[0:16, 0:512]
        gsc = ring[0:16, 512:1024]
        p1_af = afoff[0]

        aboff[0] = AB_C0; afoff[0] = AF_D0
        o = abal(2048)
        ropes = [AB[:, o:o + 2048], AB[:, o:o + 2048]]
        o = abal(4 * 2048)
        wq4 = AB[:, o:o + 2048]; wk4 = AB[:, o + 2048:o + 4096]
        wv4 = AB[:, o + 4096:o + 6144]; wg4 = AB[:, o + 6144:o + 8192]
        o = abal(4 * T)
        QTE = AB[:, o:o + T]; QTO = AB[:, o + T:o + 2 * T]
        KTE = AB[:, o + 2 * T:o + 3 * T]; KTO = AB[:, o + 3 * T:o + 4 * T]
        o = abal(nTC * 260)
        vtm = [AB[:, o + i * 260:o + (i + 1) * 260] for i in range(nTC)]
        o = abal(2 * T)
        ognq = [AB[:, o:o + T], AB[:, o + T:o + 2 * T]]
        o = abal(8 * 512)
        ptile = [AB[:, o + i * 512:o + (i + 1) * 512] for i in range(8)]
        o = abal(512); gate_t = AB[:, o:o + 512]
        o = abal(512); go_t = AB[:, o:o + 512]
        p2_ab = aboff[0]
        o = afal(4 * 512); rtmp = [AF32[:, o + i * 512:o + (i + 1) * 512] for i in range(4)]
        o = afal(4 * 512); tlo = [AF32[0:65, o + i * 512:o + (i + 1) * 512] for i in range(4)]
        o = afal(4 * 512); ogu = [AF32[0:65, o + i * 512:o + (i + 1) * 512] for i in range(4)]
        o = afal(64); sstat = AF32[:, o:o + 64]
        o = afal(4 * 512); tdel = [AF32[:, o + i * 512:o + (i + 1) * 512] for i in range(4)]
        p2_af = afoff[0]

        aboff[0] = AB_C0; afoff[0] = AF_D0
        o = abal(32 * 512); hrel = AB[:, o:o + 32 * 512]
        o = abal(2 * 1024); w1s = [AB[:, o:o + 1024], AB[:, o + 1024:o + 2048]]
        o = abal(2 * 1024); w2s = [AB[:, o:o + 1024], AB[:, o + 1024:o + 2048]]
        o = abal(8 * 1024); wos = AB[:, o:o + 8192]
        o = abal(2 * 4096); go3 = [AB[:, o:o + 4096], AB[:, o + 4096:o + 8192]]
        o = abal(NBLK * 512); mrow = AB[0:1, o:o + NBLK * 512]
        o = abal(2 * E); bbh = AB[0:1, o:o + E]; bbl = AB[0:1, o + E:o + 2 * E]
        p3_ab = aboff[0]
        o = afal(2 * E); xres = [AF32[:, o:o + E], AF32[:, o + E:o + 2 * E]]
        o = afal(2 * E); oev = [AF32[:, o:o + E], AF32[:, o + E:o + 2 * E]]
        p3_af = afoff[0]

        assert max(p1_ab, p2_ab, p3_ab) <= AB.shape[1], (p1_ab, p2_ab, p3_ab)
        assert max(p1_af, p2_af, p3_af) <= AF32.shape[1], (p1_af, p2_af, p3_af)

        with nc.Block() as block:
            CL = {}

            def cload(key, dst, src):
                CL[key] = P.dma("sp", dst, src)

            cload("ident", identT, identFd[:, :])
            cload("i28", iota28T, iota28d[:, :])
            cload("hm2", hm2_sb, hm2d[:, :])
            cload("hm2T", hm2T_sb, hm2Td[:, :])
            cload("onesf", onesf_sb, onesfd[:, :])
            cload("constf", constf, constfd[:, :])
            cload("rout", routsb.rearrange("p (c m) -> p c m", c=8),
                  routT.rearrange("(c p) m -> p c m", p=128))
            cload("wd", wdsb.rearrange("p (c m) -> p c m", c=8),
                  wdT.rearrange("(c p) m -> p c m", p=128))
            cload("bqE", bqE_sb, bqEd[:, :])
            cload("bqO", bqO_sb, bqOd[:, :])
            cload("bkE", bkE_sb, bkEd[:, :])
            cload("bkO", bkO_sb, bkOd[:, :])
            cload("bg", bg_sb, bg_cd[:, :])
            cload("b1", b1_sb, b1_cd[:, :])
            cload("gam", gam_sb, gam_cd[:, :])
            cload("bet", bet_sb, bet_cd[:, :])
            cload("brf", brow_f, browsd[:, :])
            cload("rowsm", rowsm_sb, rowsmd[:, :])
            cload("bdc", bdc_sb, bd_cd[:, :])
            cload("rbc", rbc_sb, rb_cd[:, :])
            cload("ind", indT, ind16d[:, :])

            ms1 = P.add("dve", lambda e: e.memset(ones_r, 1.0))
            s0 = P.add("dve", lambda e: e.tensor_tensor(bob2_sb, bo_sb, b2_sb, ALU.add),
                       [CL["brf"]])
            def hilo_split(hi, lo, srcrow, extra_deps):
                splits = []
                c1 = P.add("dve", lambda e, hi=hi, srcrow=srcrow: e.tensor_copy(
                    hi, srcrow), [s0, CL["brf"]] + extra_deps)
                for half in range(2):
                    sl = slice(half * 512, (half + 1) * 512)
                    d1 = P.add("dve", lambda e, hi=hi, srcrow=srcrow, sl=sl: e.tensor_tensor(
                        scr[0:1, 0:512], srcrow[0:1, sl], hi[0:1, sl], ALU.subtract),
                        [c1] + splits[-1:])
                    d2 = P.add("dve", lambda e, lo=lo, sl=sl: e.tensor_copy(
                        lo[0:1, sl], scr[0:1, 0:512]), [d1])
                    splits.append(d2)
                return splits
            bias_ready = hilo_split(bvh, bvl, bv_sb, []) + [ms1]

            precast = []
            prev_st = [None, None]
            k = 0
            for (src, dst, rows, cols) in ((w1T, w1b, E, HID), (w2T, w2b, HID, E),
                                           (woT, wob, E, E)):
                for rc in range(rows // 128):
                    bi = k % 2
                    ld = P.dma("gp", wstg[bi][:, 0:cols], src[rc * 128:(rc + 1) * 128, :],
                               [prev_st[bi]])
                    st = P.dma("sp", dst[rc * 128:(rc + 1) * 128, :], wstg[bi][:, 0:cols],
                               [ld])
                    prev_st[bi] = st
                    precast.append(st)
                    k += 1

            all_out_stores = []
            batch_tail = None

            for b in range(P2C):
                xn_stores = []
                logit_stores = []
                xpart_ops = []
                prev_xin = [batch_tail, batch_tail]
                prev_ring = [batch_tail] if batch_tail else []
                prev_ltile = batch_tail
                tr_bank_free = [batch_tail, batch_tail]
                xstg_guard = [batch_tail] * 8
                for tb in range(nTB):
                    ring_writers = []
                    for tci in range(4):
                        tc = tb * 4 + tci
                        bi = tc % 2
                        SO = 16 * bi
                        ld = P.dma("sp", xin[bi], x_in[b, tc * 128:(tc + 1) * 128, :],
                                   [prev_xin[bi]])
                        r1 = P.add("dve", lambda e, bi=bi, SO=SO: e.reduce_sum(
                            stat[:, SO + 0:SO + 1], xin[bi], axis=AX.X), [ld])
                        r2 = P.add("act", lambda e, bi=bi, SO=SO: e.activation(
                            trash_b[:, 0:512], xin[bi][:, 0:512], AF.Square,
                            accum_out=stat[:, SO + 1:SO + 2]), [ld])
                        r2b = P.add("act", lambda e, bi=bi, SO=SO: e.activation(
                            trash_b[:, 0:512], xin[bi][:, 512:1024], AF.Square,
                            accum_out=stat[:, SO + 2:SO + 3]), [r2])
                        t0 = P.add("dve", lambda e, SO=SO: e.tensor_tensor(
                            stat[:, SO + 3:SO + 4], stat[:, SO + 1:SO + 2], stat[:, SO + 2:SO + 3], ALU.add), [r2b])
                        t1 = P.add("dve", lambda e, SO=SO: e.tensor_single_scalar(
                            stat[:, SO + 4:SO + 5], stat[:, SO + 0:SO + 1], 1.0 / E, ALU.mult), [r1])
                        t2 = P.add("dve", lambda e, SO=SO: e.tensor_tensor(
                            stat[:, SO + 5:SO + 6], stat[:, SO + 4:SO + 5], stat[:, SO + 4:SO + 5], ALU.mult), [t1])
                        t3 = P.add("dve", lambda e, SO=SO: e.scalar_tensor_tensor(
                            stat[:, SO + 6:SO + 7], stat[:, SO + 3:SO + 4], 1.0 / E, stat[:, SO + 5:SO + 6],
                            ALU.mult, ALU.subtract), [t0, t2])
                        a1 = P.add("act", lambda e, SO=SO: e.activation(
                            stat[:, SO + 7:SO + 8], stat[:, SO + 6:SO + 7], AF.Ln,
                            bias=np.float32(1e-5)), [t3])
                        a2 = P.add("act", lambda e, SO=SO: e.activation(
                            stat[:, SO + 8:SO + 9], stat[:, SO + 7:SO + 8], AF.Exp, scale=-0.5), [a1])
                        t4 = P.add("dve", lambda e, SO=SO: e.tensor_tensor(
                            stat[:, SO + 9:SO + 10], stat[:, SO + 4:SO + 5], stat[:, SO + 8:SO + 9], ALU.mult), [a2])
                        t5 = P.add("dve", lambda e, SO=SO: e.tensor_single_scalar(
                            stat[:, SO + 10:SO + 11], stat[:, SO + 9:SO + 10], -1.0, ALU.mult), [t4])
                        def _lnapply(e, bi=bi, SO=SO):
                            ob, bb = bass.broadcast_tensor_aps(xn32[bi], stat[:, SO + 10:SO + 11])
                            return e.scalar_tensor_tensor(
                                xn32[bi], xin[bi], stat[:, SO + 8:SO + 9], bb,
                                ALU.mult, ALU.add)
                        ap = P.add("dve", _lnapply, [t5])
                        prev_xin[bi] = ap
                        for ec in range(8):
                            bank = pb[ec % 2]
                            tr = P.add("pe", lambda e, bi=bi, ec=ec, bank=bank: e.transpose(
                                bank[:, 0:128], xn32[bi][:, ec * 128:(ec + 1) * 128],
                                identT),
                                [ap, CL["ident"], tr_bank_free[ec % 2]]
                                + (prev_ring if (tci == 0 and ec < 2) else []))
                            cp = P.add("dve", lambda e, ec=ec, tci=tci, bank=bank: e.tensor_copy(
                                ring[:, ec * 512 + tci * 128:ec * 512 + tci * 128 + 128],
                                bank[:, 0:128]), [tr])
                            tr_bank_free[ec % 2] = cp
                            ring_writers.append(cp)
                    ring_readers = []
                    last_rmm = None
                    for ec in range(8):
                        deps = ring_writers[ec::8]
                        cst = P.add("dve", lambda e, ec=ec: e.tensor_copy(
                            xstg[:, ec * 512:(ec + 1) * 512],
                            ring[:, ec * 512:(ec + 1) * 512]),
                            deps + [xstg_guard[ec]])
                        st = P.dma("sp", xnTs[b].rearrange("(c p) t -> p c t", p=128)[
                            :, ec, tb * 512:(tb + 1) * 512],
                            xstg[:, ec * 512:(ec + 1) * 512], [cst])
                        xn_stores.append(st)
                        xstg_guard[ec] = st
                        xp = P.add("dve", lambda e, ec=ec, tb=tb: e.reduce_sum(
                            xpart[:, ec * 8 + tb:ec * 8 + tb + 1],
                            ring[:, ec * 512:(ec + 1) * 512], axis=AX.X), deps)
                        xpart_ops.append(xp)
                        mm = P.add("pe", lambda e, ec=ec: e.matmul(
                            pb[2][0:16, 0:512], routsb[:, ec * 16:(ec + 1) * 16],
                            ring[:, ec * 512:(ec + 1) * 512],
                            start=(ec == 0), stop=(ec == 7)),
                            deps + [CL["rout"], prev_ltile])
                        last_rmm = mm
                        ring_readers.extend([st, xp, mm])
                    le = P.add("dve", lambda e: e.tensor_single_scalar(
                        ltile, pb[2][0:16, 0:512], rbc_sb, ALU.add),
                        [last_rmm, prev_ltile, CL["rbc"]])
                    lst = P.dma("sp", logitD[b][:, tb * 512:(tb + 1) * 512], ltile, [le])
                    logit_stores.append(lst)
                    prev_ltile = lst
                    prev_ring = ring_readers + [le]

                xm_ops = []
                for ec in range(8):
                    m1 = P.add("dve", lambda e, ec=ec: e.reduce_sum(
                        stat[:, 11:12], xpart[:, ec * 8:ec * 8 + nTB], axis=AX.X),
                        xpart_ops[ec::8])
                    m2 = P.add("dve", lambda e, ec=ec: e.tensor_single_scalar(
                        xmean[:, ec:ec + 1], stat[:, 11:12], 1.0 / T, ALU.mult), [m1])
                    xm_ops.append(m2)
                pdr = None
                for ec in range(8):
                    pdr = P.add("pe", lambda e, ec=ec: e.matmul(
                        pb[3][0:1, 0:H], xmean[:, ec:ec + 1], wdsb[:, ec * H:(ec + 1) * H],
                        start=(ec == 0), stop=(ec == 7)), [xm_ops[ec], CL["wd"]])
                pdc = None
                for ec in range(8):
                    pdc = P.add("pe", lambda e, ec=ec: e.matmul(
                        pb[4][0:H, 0:1], wdsb[:, ec * H:(ec + 1) * H], xmean[:, ec:ec + 1],
                        start=(ec == 0), stop=(ec == 7)), [xm_ops[ec], CL["wd"]])
                r0 = P.add("dve", lambda e: e.tensor_tensor(
                    scr[0:1, 0:H], pb[3][0:1, 0:H], bd_sb, ALU.add), [pdr, CL["rowsm"]])
                r1 = P.add("act", lambda e: e.activation(
                    scr[0:1, 16:32], scr[0:1, 0:16], AF.Sigmoid), [r0])
                r2 = P.add("act", lambda e: e.activation(
                    scr[0:1, 32:48], scr[0:1, 16:32], AF.Exp,
                    scale=-8.0 * LN2, bias=np.float32(-5.0 * LN2)), [r1])
                r3 = P.add("dve", lambda e: e.tensor_scalar(
                    scr[0:1, 48:64], scr[0:1, 32:48], -1.0, 1.0, ALU.mult, ALU.add),
                    [r2])
                r4 = P.add("act", lambda e: e.activation(
                    ldrow, scr[0:1, 48:64], AF.Ln), [r3])
                c0 = P.add("dve", lambda e: e.tensor_tensor(
                    stat[0:H, 12:13], pb[4][0:H, 0:1], bdc_sb, ALU.add),
                    [pdc, CL["bdc"]])
                c1 = P.add("act", lambda e: e.activation(
                    stat[0:H, 13:14], stat[0:H, 12:13], AF.Sigmoid), [c0])
                c2 = P.add("act", lambda e: e.activation(
                    stat[0:H, 14:15], stat[0:H, 13:14], AF.Exp,
                    scale=-8.0 * LN2, bias=np.float32(-5.0 * LN2)), [c1])
                c3 = P.add("dve", lambda e: e.tensor_scalar(
                    stat[0:H, 15:16], stat[0:H, 14:15], -1.0, 1.0, ALU.mult, ALU.add),
                    [c2])
                c4 = P.add("act", lambda e: e.activation(
                    ldcol, stat[0:H, 15:16], AF.Ln), [c3])
                mm1 = P.add("pe", lambda e: e.matmul(
                    pb[3][:, 0:H], onesf_sb, ldrow, start=True, stop=True),
                    [r4, CL["onesf"], r0])
                cp1 = P.add("dve", lambda e: e.tensor_copy(ldbc, pb[3][:, 0:H]), [mm1])
                cp2 = P.add("dve", lambda e: e.tensor_single_scalar(
                    ld8bc, ldbc, 8.0, ALU.mult), [cp1])
                hib_ops = []
                for h in range(H):
                    hib_ops.append(P.add("dve", lambda e, h=h: e.tensor_scalar(
                        hibias[:, h * nDELTA:(h + 1) * nDELTA], iota28T,
                        ldbc[:, h:h + 1], -C_SHIFT, ALU.mult, ALU.add),
                        [cp1, CL["i28"]]))
                ldq = P.dma("sp", iotaq_sb, iotaqd[:, :],
                            [xn_stores[-1], logit_stores[-1]])
                g0 = P.add("dve", lambda e: e.tensor_single_scalar(
                    gsc, iotaq_sb, ldcol, ALU.mult), [c4, ldq])
                g1 = P.add("act", lambda e: e.activation(g16, gsc, AF.Exp), [g0])

                mask_ops = []
                prev_tr = c0
                prev_lload = None
                for grp in range(nTC // 4):
                    ll = P.dma("sp", ltile, logitD[b][:, grp * 512:(grp + 1) * 512],
                               [logit_stores[grp], prev_lload])
                    tr_cp = []
                    for i in range(4):
                        tr = P.add("pe", lambda e, i=i: e.transpose(
                            pb[5][:, 0:16], ltile[0:16, i * 128:(i + 1) * 128],
                            identT[0:16, 0:16]), [ll, prev_tr])
                        cp = P.add("dve", lambda e, i=i: e.tensor_copy(
                            ltm[:, i * 16:(i + 1) * 16], pb[5][:, 0:16]), [tr])
                        prev_tr = cp
                        tr_cp.append(cp)
                    cur = ltm[:, 0:64].rearrange("p (i j) -> p i j", j=16)
                    ge = mbtm[:, 0:64].rearrange("p (i j) -> p i j", j=16)
                    prev = tr_cp
                    for it in range(4):
                        m1 = P.add("dve", lambda e: e.tensor_reduce(
                            mxs[:, 0:4], cur, axis=AX.X, op=ALU.max), prev)
                        mxb = mxs[:, 0:4].rearrange("p (i j) -> p i j", j=1)
                        _, mxbb = bass.broadcast_tensor_aps(cur, mxb)
                        g2 = P.add("dve", lambda e, mxbb=mxbb: e.tensor_tensor(
                            ge, cur, mxbb, ALU.is_ge), [m1])
                        u1 = P.add("dve", lambda e: e.scalar_tensor_tensor(
                            cur, ge, -1e30, cur, ALU.mult, ALU.add), [g2])
                        prev = [u1]
                    fm = P.add("dve", lambda e: e.tensor_single_scalar(
                        mbtm[:, 64:128], ltm[:, 0:64], -1e29, ALU.is_le), prev)
                    mb = P.add("dve", lambda e: e.tensor_scalar(
                        mbtm[:, 128:192], mbtm[:, 64:128], BIG, -BIG, ALU.mult,
                        ALU.add), [fm])
                    for i in range(4):
                        tc = grp * 4 + i
                        tr = P.add("pe", lambda e, i=i: e.transpose(
                            pb[5][0:16, 0:128], mbtm[:, 128 + i * 16:128 + (i + 1) * 16],
                            identT), [mb, prev_tr])
                        cp = P.add("dve", lambda e, tc=tc: e.tensor_copy(
                            maskb[0:16, tc * 128:(tc + 1) * 128], pb[5][0:16, 0:128]),
                            [tr])
                        prev_tr = cp
                        mask_ops.append(cp)
                    prev_lload = tr_cp[-1]
                mst = P.dma("sp", maskD[b], maskb, mask_ops)
                p15_tail = mask_ops[-1]
                tables_done = hib_ops + [cp2, g1, p15_tail]

                go_stores = []
                td_loads = []
                for d in range(4):
                    td_loads.append(P.dma("sp", tdel[d], tdeltad[:, d * 512:(d + 1) * 512],
                                          [p15_tail, mst]))
                quad_tail = None
                prev_wl = None
                for q in range(nQ):
                    wdeps = (list(xn_stores) + [p15_tail, mst]) if q == 0 else [quad_tail, prev_wl]
                    wl = []
                    for (wsb, wdram) in ((wq4, wqT), (wk4, wkT), (wv4, wvT), (wg4, wgT)):
                        wl.append(P.dma("gp",
                                        wsb.rearrange("p (c m) -> p c m", c=8),
                                        wdram.rearrange("(c p) m -> p c m", p=128)[
                                            :, :, q * 256:(q + 1) * 256], wdeps))
                    prev_wl = wl[3]

                    rope_done = []
                    v_done = []
                    rope_guard = [quad_tail]
                    prev_xl = [quad_tail, quad_tail]
                    bank01_free = quad_tail
                    rtmp_guard = [quad_tail] * 4
                    for tb in range(nTB):
                        bi = tb % 2
                        xdeps = [prev_xl[bi]] + (list(xn_stores) if (q == 0 and tb < 2) else [])
                        xl = P.dma("sp",
                                   xnt[bi].rearrange("p (c t) -> p c t", c=8),
                                   xnTs[b].rearrange("(c p) t -> p c t", p=128)[
                                       :, :, tb * 512:(tb + 1) * 512], xdeps)
                        rl = P.dma("sp",
                                   ropes[bi].rearrange("p (f t) -> p f t", f=4),
                                   ropetabd.rearrange("p (f t) -> p f t", f=4)[
                                       :, :, tb * 512:(tb + 1) * 512],
                                   [prev_xl[bi], rope_guard[0]])
                        cosE_s = ropes[bi][:, 0:512]; nsinE_s = ropes[bi][:, 512:1024]
                        cosO_s = ropes[bi][:, 1024:1536]; sinO_s = ropes[bi][:, 1536:2048]
                        tb_users = []
                        sl = slice(tb * 512, (tb + 1) * 512)
                        for (wsb, wdep, bEc, bOc, dstE, dstO) in (
                                (wq4, wl[0], bqE_sb, bqO_sb, QTE, QTO),
                                (wk4, wl[1], bkE_sb, bkO_sb, KTE, KTO)):
                            mmE = None
                            for ec in range(8):
                                mmE = P.add("pe", lambda e, ec=ec, wsb=wsb, bi=bi: e.matmul(
                                    pb[0][:, 0:512], wsb[:, ec * 256:ec * 256 + 128],
                                    xnt[bi][:, ec * 512:(ec + 1) * 512],
                                    start=(ec == 0), stop=(ec == 7)),
                                    [xl, wdep] + ([bank01_free] if ec == 0 else []))
                            mmO = None
                            for ec in range(8):
                                mmO = P.add("pe", lambda e, ec=ec, wsb=wsb, bi=bi: e.matmul(
                                    pb[1][:, 0:512], wsb[:, ec * 256 + 128:ec * 256 + 256],
                                    xnt[bi][:, ec * 512:(ec + 1) * 512],
                                    start=(ec == 0), stop=(ec == 7)),
                                    [xl, wdep] + ([bank01_free] if ec == 0 else []))
                            e1 = P.add("dve", lambda e, bEc=bEc, q=q, cosE_s=cosE_s: e.scalar_tensor_tensor(
                                rtmp[0], pb[0][:, 0:512], bEc[:, q:q + 1], cosE_s,
                                ALU.add, ALU.mult),
                                [mmE, rl, CL["bqE"], CL["bkE"], rtmp_guard[0]])
                            e2 = P.add("dve", lambda e, bOc=bOc, q=q, nsinE_s=nsinE_s: e.scalar_tensor_tensor(
                                rtmp[1], pb[1][:, 0:512], bOc[:, q:q + 1], nsinE_s,
                                ALU.add, ALU.mult),
                                [mmO, rl, CL["bqO"], CL["bkO"], rtmp_guard[1]])
                            e3 = P.add("dve", lambda e, dstE=dstE, sl=sl: e.tensor_tensor(
                                dstE[:, sl], rtmp[0], rtmp[1], ALU.add), [e1, e2])
                            e4 = P.add("dve", lambda e, bOc=bOc, q=q, cosO_s=cosO_s: e.scalar_tensor_tensor(
                                rtmp[2], pb[1][:, 0:512], bOc[:, q:q + 1], cosO_s,
                                ALU.add, ALU.mult), [mmO, rtmp_guard[2]])
                            e5 = P.add("dve", lambda e, bEc=bEc, q=q, sinO_s=sinO_s: e.scalar_tensor_tensor(
                                rtmp[3], pb[0][:, 0:512], bEc[:, q:q + 1], sinO_s,
                                ALU.add, ALU.mult), [mmE, rtmp_guard[3]])
                            e6 = P.add("dve", lambda e, dstO=dstO, sl=sl: e.tensor_tensor(
                                dstO[:, sl], rtmp[2], rtmp[3], ALU.add), [e4, e5])
                            rope_done.extend([e3, e6])
                            tb_users.extend([e3, e6])
                            bank01_free = e6
                            rope_guard[0] = e6
                        prev_vstep = None
                        for tci in range(4):
                            tc = tb * 4 + tci
                            mh = P.add("pe", lambda e, q=q: e.matmul(
                                pb[2][:, 0:256], ones_r, bvh[0:1, q * 256:(q + 1) * 256],
                                start=True, stop=False),
                                bias_ready + [xl] + ([v_done[-1]] if v_done else [quad_tail]))
                            mlo = P.add("pe", lambda e, q=q: e.matmul(
                                pb[2][:, 0:256], ones_r, bvl[0:1, q * 256:(q + 1) * 256],
                                start=False, stop=False), [mh])
                            mmv = None
                            for ec in range(8):
                                mmv = P.add("pe", lambda e, ec=ec, tci=tci, bi=bi: e.matmul(
                                    pb[2][:, 0:256],
                                    xnt[bi][:, ec * 512 + tci * 128:ec * 512 + tci * 128 + 128],
                                    wv4[:, ec * 256:(ec + 1) * 256],
                                    start=False, stop=(ec == 7)),
                                    [xl, wl[2], mlo])
                            vc = P.add("dve", lambda e, tc=tc: e.tensor_copy(
                                vtm[tc].rearrange("p (h d) -> p h d", d=65)[:, :, 0:64],
                                pb[2][:, 0:256].rearrange("p (h d) -> p h d", d=64)),
                                [mmv])
                            vo = P.add("dve", lambda e, tc=tc: e.memset(
                                vtm[tc].rearrange("p (h d) -> p h d", d=65)[:, :, 64:65],
                                1.0), [vc])
                            v_done.append(vo)
                            tb_users.append(vo)
                        prev_xl[bi] = tb_users[-1]

                    gb_done = []
                    prev_gb = cp2 if q == 0 else quad_tail
                    for j in range(4):
                        h = 4 * q + j
                        mm = P.add("pe", lambda e, h=h: e.matmul(
                            pb[3][:, 0:512], indT[0:16, h * 128:(h + 1) * 128], g16,
                            start=True, stop=True), [g1, CL["ind"], prev_gb])
                        cp = P.add("dve", lambda e, j=j: e.tensor_copy(
                            gbh4[j], pb[3][:, 0:512]), [mm])
                        prev_gb = cp
                        gb_done.append(cp)

                    qkv_done = rope_done + v_done + tables_done + td_loads + gb_done

                    ogn_written = []
                    stat_ops = []
                    og_free = [quad_tail] * 4
                    s_free = [quad_tail] * 4
                    pt_free = [quad_tail] * 8
                    exp_rtmp_guard = [None] * 4
                    for qb in range(nQB):
                        q0 = qb * 512
                        kd0 = q0 // 128
                        av_last = [None] * 4
                        tlo_ops = [None] * 4

                        def scores(k, extra):
                            res = []
                            for j in range(4):
                                mmE = P.add("pe", lambda e, j=j, k=k, q0=q0: e.matmul(
                                    pb[4 + j][:, 0:512],
                                    KTE[32 * j:32 * j + 32, k * 128:(k + 1) * 128],
                                    QTE[32 * j:32 * j + 32, q0:q0 + 512],
                                    start=True, stop=False, tile_position=(32 * j, 0)),
                                    [s_free[j]] + extra)
                                mmO = P.add("pe", lambda e, j=j, k=k, q0=q0: e.matmul(
                                    pb[4 + j][:, 0:512],
                                    KTO[32 * j:32 * j + 32, k * 128:(k + 1) * 128],
                                    QTO[32 * j:32 * j + 32, q0:q0 + 512],
                                    start=False, stop=True, tile_position=(32 * j, 0)),
                                    [mmE])
                                res.append(mmO)
                            return res

                        def exp_av(k, kind, sc_ops, first_av, last_av):
                            for j in range(4):
                                h = 4 * q + j
                                slot = (k % 2) * 4 + j
                                pt = ptile[slot]
                                ptdep = pt_free[slot]
                                if kind == "diag":
                                    d = k - kd0
                                    stx = P.add("dve", lambda e, j=j, d=d, h=h: e.scalar_tensor_tensor(
                                        rtmp[j], tdel[d], ld8bc[:, h:h + 1],
                                        pb[4 + j][:, 0:512], ALU.mult, ALU.add),
                                        [sc_ops[j], td_loads[d], cp2,
                                         exp_rtmp_guard[j]])
                                    ex = P.add("act", lambda e, pt=pt, j=j: e.activation(
                                        pt, rtmp[j], AF.Exp, scale=0.125,
                                        bias=np.float32(-C_SHIFT)), [stx, ptdep])
                                    exp_rtmp_guard[j] = ex
                                    s_free[j] = stx
                                elif kind == "hi":
                                    d = k - kd0 - 4
                                    ex = P.add("act", lambda e, pt=pt, j=j, h=h, d=d: e.activation(
                                        pt, pb[4 + j][:, 0:512], AF.Exp, scale=0.125,
                                        bias=hibias[:, h * nDELTA + d:h * nDELTA + d + 1]),
                                        [sc_ops[j], ptdep, hib_ops[h]])
                                    s_free[j] = ex
                                else:
                                    ex = P.add("act", lambda e, pt=pt, j=j: e.activation(
                                        pt, pb[4 + j][:, 0:512], AF.Exp, scale=0.125,
                                        bias=np.float32(-C_SHIFT)), [sc_ops[j], ptdep])
                                    s_free[j] = ex
                                avdeps = [ex, av_last[j]]
                                if first_av:
                                    avdeps.append(og_free[j])
                                mm = P.add("pe", lambda e, j=j, k=k, pt=pt, first_av=first_av, last_av=last_av: e.matmul(
                                    pb[j][0:65, 0:512], vtm[k][:, j * 65:j * 65 + 65],
                                    pt, start=first_av, stop=last_av), avdeps)
                                av_last[j] = mm
                                pt_free[slot] = mm

                        for k in range(0, kd0 + 4):
                            kind = "diag" if k >= kd0 else "lo"
                            sc = scores(k, qkv_done if (qb == 0 and k < 2) else [])
                            exp_av(k, kind, sc, first_av=(k == 0),
                                   last_av=(k == kd0 + 3))
                        for j in range(4):
                            tl = P.add("dve", lambda e, j=j: e.tensor_tensor(
                                tlo[j], pb[j][0:65, 0:512], gbh4[j][0:65, 0:512],
                                ALU.mult), [av_last[j], gb_done[j]])
                            tlo_ops[j] = tl
                            og_free[j] = tl
                        av_last = [None] * 4
                        has_hi = kd0 + 4 < nKC
                        for k in range(kd0 + 4, nKC):
                            sc = scores(k, [])
                            exp_av(k, "hi", sc, first_av=(k == kd0 + 4),
                                   last_av=(k == nKC - 1))
                        for j in range(4):
                            h = 4 * q + j
                            if has_hi:
                                cb = P.add("dve", lambda e, j=j: e.tensor_tensor(
                                    ogu[j], pb[j][0:65, 0:512], tlo[j], ALU.add),
                                    [av_last[j], tlo_ops[j]])
                                og_free[j] = cb
                            else:
                                cb = P.add("dve", lambda e, j=j: e.tensor_copy(
                                    ogu[j], tlo[j]), [tlo_ops[j]])
                            rc = P.add("dve", lambda e, j=j: e.reciprocal(
                                ogu[j][64:65, 0:512], ogu[j][64:65, 0:512]), [cb])
                            rbc = P.add("dve", lambda e, j=j: e.tensor_copy(
                                ptile[j][0:1, 0:512], ogu[j][64:65, 0:512]),
                                [rc, pt_free[j]])
                            zb = P.add("pe", lambda e, j=j: e.matmul(
                                pb[4 + j][:, 0:512], ones_r, ptile[j][0:1, 0:512],
                                start=True, stop=True), [rbc, s_free[j], ms1])
                            pt_free[j] = zb
                            ogn_dst = ognq[j // 2][64 * (j % 2):64 * (j % 2) + 64,
                                                  q0:q0 + 512]
                            nm = P.add("dve", lambda e, j=j, ogn_dst=ogn_dst: e.tensor_tensor(
                                ogn_dst, ogu[j][0:64, 0:512], pb[4 + j][0:64, 0:512],
                                ALU.mult), [zb])
                            s_free[j] = nm
                            ogn_written.append(nm)
                            sp_ = P.add("dve", lambda e, j=j, qb=qb, ogn_dst=ogn_dst: e.reduce_sum(
                                sstat[64 * (j % 2):64 * (j % 2) + 64,
                                      (j // 2) * 16 + qb:(j // 2) * 16 + qb + 1],
                                ogn_dst, axis=AX.X), [nm])
                            sq_ = P.add("act", lambda e, j=j, qb=qb, ogn_dst=ogn_dst: e.activation(
                                gate_t[0:64, 0:512], ogn_dst, AF.Square,
                                accum_out=sstat[64 * (j % 2):64 * (j % 2) + 64,
                                                (j // 2) * 16 + 8 + qb:(j // 2) * 16 + 8 + qb + 1]),
                                [nm])
                            stat_ops.extend([sp_, sq_])

                    prev_xl2 = prev_xl
                    bank3_guard = gb_done[-1]
                    go_tail = None
                    prev_gm = None
                    for pc in range(2):
                        su = P.add("dve", lambda e, pc=pc: e.reduce_sum(
                            sstat[:, 32 + 4 * pc:33 + 4 * pc],
                            sstat[:, pc * 16:pc * 16 + nQB], axis=AX.X), stat_ops)
                        qu = P.add("dve", lambda e, pc=pc: e.reduce_sum(
                            sstat[:, 33 + 4 * pc:34 + 4 * pc],
                            sstat[:, pc * 16 + 8:pc * 16 + 8 + nQB], axis=AX.X),
                            stat_ops)
                        m1 = P.add("pe", lambda e, pc=pc: e.matmul(
                            pb[2][0:2, 0:2], hm2_sb,
                            sstat[:, 32 + 4 * pc:34 + 4 * pc],
                            start=True, stop=True),
                            [su, qu, CL["hm2"], v_done[-1]])
                        t1 = P.add("dve", lambda e: e.tensor_single_scalar(
                            sstat[0:2, 40:42], pb[2][0:2, 0:2], 1.0 / (T * 64.0),
                            ALU.mult), [m1])
                        t2 = P.add("dve", lambda e: e.tensor_tensor(
                            sstat[0:2, 42:43], sstat[0:2, 40:41], sstat[0:2, 40:41],
                            ALU.mult), [t1])
                        t3 = P.add("dve", lambda e: e.tensor_tensor(
                            sstat[0:2, 43:44], sstat[0:2, 41:42], sstat[0:2, 42:43],
                            ALU.subtract), [t2])
                        a1 = P.add("act", lambda e: e.activation(
                            sstat[0:2, 44:45], sstat[0:2, 43:44], AF.Ln,
                            bias=np.float32(1e-5)), [t3])
                        a2 = P.add("act", lambda e: e.activation(
                            sstat[0:2, 45:46], sstat[0:2, 44:45], AF.Exp, scale=-0.5),
                            [a1])
                        m2 = P.add("pe", lambda e: e.matmul(
                            pb[3][:, 0:1], hm2T_sb, sstat[0:2, 40:41],
                            start=True, stop=True), [t1, a2, bank3_guard])
                        m3 = P.add("pe", lambda e: e.matmul(
                            pb[3][:, 1:2], hm2T_sb, sstat[0:2, 45:46],
                            start=False, stop=True, skip_group_check=True), [m2])
                        gidx = q * 2 + pc
                        A1 = P.add("dve", lambda e, gidx=gidx: e.tensor_tensor(
                            sstat[:, 48:49], pb[3][:, 1:2], gam_sb[:, gidx:gidx + 1],
                            ALU.mult), [m3, CL["gam"]])
                        B1 = P.add("dve", lambda e: e.tensor_tensor(
                            sstat[:, 49:50], pb[3][:, 0:1], sstat[:, 48:49], ALU.mult),
                            [A1])
                        B2 = P.add("dve", lambda e, gidx=gidx: e.tensor_tensor(
                            sstat[:, 50:51], bet_sb[:, gidx:gidx + 1], sstat[:, 49:50],
                            ALU.subtract), [B1, CL["bet"]])
                        prev_gate_bank = B1
                        for tb2 in range(nTB):
                            bi2 = tb2 % 2
                            xl2 = P.dma("sp",
                                        xnt[bi2].rearrange("p (c t) -> p c t", c=8),
                                        xnTs[b].rearrange("(c p) t -> p c t", p=128)[
                                            :, :, tb2 * 512:(tb2 + 1) * 512],
                                        [prev_xl2[bi2]])
                            mmg = None
                            for ec in range(8):
                                mmg = P.add("pe", lambda e, ec=ec, pc=pc, bi2=bi2: e.matmul(
                                    pb[3][:, 0:512],
                                    wg4[:, ec * 256 + pc * 128:ec * 256 + pc * 128 + 128],
                                    xnt[bi2][:, ec * 512:(ec + 1) * 512],
                                    start=(ec == 0), stop=(ec == 7)),
                                    [xl2, wl[3]] + ([prev_gate_bank] if ec == 0 else []))
                            prev_xl2[bi2] = mmg
                            ga = P.add("act", lambda e, gidx=gidx: e.activation(
                                gate_t, pb[3][:, 0:512], AF.Silu,
                                bias=bg_sb[:, gidx:gidx + 1]),
                                [mmg, CL["bg"], prev_gm])
                            prev_gate_bank = ga
                            sl2 = slice(tb2 * 512, (tb2 + 1) * 512)
                            def _affine(e, pc=pc, sl2=sl2):
                                ob, bb = bass.broadcast_tensor_aps(
                                    go_t, sstat[:, 50:51])
                                return e.scalar_tensor_tensor(
                                    go_t, ognq[pc][:, sl2], sstat[:, 48:49], bb,
                                    ALU.mult, ALU.add)
                            af1 = P.add("dve", _affine,
                                        [B2, prev_gm] + (ogn_written if tb2 == 0 else []))
                            gm = P.add("dve", lambda e: e.tensor_tensor(
                                go_t, go_t, gate_t, ALU.mult), [af1, ga, prev_gm])
                            st = P.dma("sp", goTs[b].rearrange("(c p) t -> p c t", p=128)[
                                :, q * 2 + pc, tb2 * 512:(tb2 + 1) * 512], go_t, [gm])
                            prev_gm = st
                            go_stores.append(st)
                        bank3_guard = prev_gate_bank
                        go_tail = prev_gm
                    quad_tail = go_tail

                wol = P.dma("sp", wos.rearrange("p (c m) -> p c m", c=8),
                            wob.rearrange("(c p) m -> p c m", p=128),
                            precast + [quad_tail])
                bias3_ready = hilo_split(bbh, bbl, bob2_sb, [quad_tail])
                w1_prev = [quad_tail, quad_tail]
                w2_prev = [quad_tail, quad_tail]
                go_prev = [quad_tail, quad_tail]
                xn_prev = [quad_tail, quad_tail]
                xr_prev = [None, None]
                obank_free = [quad_tail] * 8
                mrow_guard = quad_tail
                out_stores = []
                for tb in range(nTB):
                    bi = tb % 2
                    xl = P.dma("sp",
                               xnt[bi].rearrange("p (c t) -> p c t", c=8),
                               xnTs[b].rearrange("(c p) t -> p c t", p=128)[
                                   :, :, tb * 512:(tb + 1) * 512], [xn_prev[bi]])
                    gl = P.dma("sp",
                               go3[bi].rearrange("p (c t) -> p c t", c=8),
                               goTs[b].rearrange("(c p) t -> p c t", p=128)[
                                   :, :, tb * 512:(tb + 1) * 512],
                               [go_prev[bi]] + (go_stores if tb < 2 else []))
                    ml = P.dma("sp",
                               mrow.rearrange("o (n t) -> o n t", n=NBLK),
                               maskD[b:b + 1, :, tb * 512:(tb + 1) * 512],
                               [mst, mrow_guard])
                    h_done = []
                    for hc in range(HID // 128):
                        wbi = hc % 2
                        wld = P.dma("sp",
                                    w1s[wbi].rearrange("p (c m) -> p c m", c=8),
                                    w1b.rearrange("(c p) m -> p c m", p=128)[
                                        :, :, hc * 128:(hc + 1) * 128],
                                    [w1_prev[wbi]] + (precast if (tb == 0 and hc < 2) else []))
                        mk = P.add("pe", lambda e, hc=hc: e.matmul(
                            pb[0][:, 0:512], ones_r,
                            mrow[0:1, (hc // 2) * 512:(hc // 2) * 512 + 512],
                            start=True, stop=False),
                            [ml, ms1, h_done[-1] if h_done else obank_free[7]])
                        mm = None
                        for ec in range(8):
                            mm = P.add("pe", lambda e, ec=ec, wbi=wbi, bi=bi: e.matmul(
                                pb[0][:, 0:512], w1s[wbi][:, ec * 128:(ec + 1) * 128],
                                xnt[bi][:, ec * 512:(ec + 1) * 512],
                                start=False, stop=(ec == 7)),
                                [wld, xl] + ([mk] if ec == 0 else []))
                        hv = P.add("dve", lambda e, hc=hc: e.tensor_scalar(
                            hrel[:, hc * 512:(hc + 1) * 512], pb[0][:, 0:512],
                            b1_sb[:, hc:hc + 1], 0.0, ALU.add, ALU.max),
                            [mm, CL["b1"]])
                        h_done.append(hv)
                        w1_prev[wbi] = hv
                    mrow_guard = mk
                    obank = [pb[i] for i in (1, 2, 3, 4, 5, 6, 7, 0)]
                    mm_last = [None] * 8
                    for ti in range(8):
                        tci, eh = ti // 2, ti % 2
                        deps0 = [obank_free[ti]] + bias3_ready
                        if ti == 7:
                            deps0.append(h_done[-1])
                        bh = P.add("pe", lambda e, eh=eh, ti=ti: e.matmul(
                            obank[ti][:, 0:512], ones_r, bbh[0:1, eh * 512:(eh + 1) * 512],
                            start=True, stop=False), deps0)
                        bl = P.add("pe", lambda e, eh=eh, ti=ti: e.matmul(
                            obank[ti][:, 0:512], ones_r, bbl[0:1, eh * 512:(eh + 1) * 512],
                            start=False, stop=False), [bh])
                        mm_last[ti] = bl
                    for hc in range(HID // 128):
                        wbi = hc % 2
                        w2l = P.dma("sp", w2s[wbi], w2b[hc * 128:(hc + 1) * 128, :],
                                    [w2_prev[wbi]] + (precast if (tb == 0 and hc < 2) else []))
                        for ti in range(8):
                            tci, eh = ti // 2, ti % 2
                            deps = [w2l, h_done[hc], mm_last[ti]]
                            mm_last[ti] = P.add("pe", lambda e, hc=hc, tci=tci, eh=eh, ti=ti, wbi=wbi: e.matmul(
                                obank[ti][:, 0:512],
                                hrel[:, hc * 512 + tci * 128:hc * 512 + tci * 128 + 128],
                                w2s[wbi][:, eh * 512:(eh + 1) * 512],
                                start=False, stop=False), deps)
                        w2_prev[wbi] = mm_last[7]
                    for ti in range(8):
                        tci, eh = ti // 2, ti % 2
                        for ec in range(8):
                            mm_last[ti] = P.add("pe", lambda e, ec=ec, tci=tci, eh=eh, ti=ti, bi=bi: e.matmul(
                                obank[ti][:, 0:512],
                                go3[bi][:, ec * 512 + tci * 128:ec * 512 + tci * 128 + 128],
                                wos[:, ec * 1024 + eh * 512:ec * 1024 + (eh + 1) * 512],
                                start=False, stop=(ec == 7)), [gl, wol, mm_last[ti]])
                    for tci in range(4):
                        tc = tb * 4 + tci
                        xr = P.dma("sp", xres[tci % 2],
                                   x_in[b, tc * 128:(tc + 1) * 128, :],
                                   [xr_prev[tci % 2]])
                        evs = []
                        for eh in range(2):
                            ti = tci * 2 + eh
                            ev = P.add("dve", lambda e, tci=tci, eh=eh, ti=ti: e.tensor_tensor(
                                oev[tci % 2][:, eh * 512:(eh + 1) * 512],
                                obank[ti][:, 0:512],
                                xres[tci % 2][:, eh * 512:(eh + 1) * 512], ALU.add),
                                [mm_last[ti], xr])
                            evs.append(ev)
                            obank_free[ti] = ev
                        st = P.dma("sp", out[b, tc * 128:(tc + 1) * 128, :],
                                   oev[tci % 2], evs)
                        xr_prev[tci % 2] = st
                        out_stores.append(st)
                    xn_prev[bi] = h_done[-1]
                    go_prev[bi] = mm_last[7]
                all_out_stores.extend(out_stores)
                batch_tail = out_stores[-1]

            P.add("sp", lambda e: e.nop(), all_out_stores)
            P.emit_into(block, sems, sim_safe=sim_safe, paranoid=paranoid)

    return nc



def _host_prep(inputs, T=4096, ncores=NCORES, p2c=2):
    g = {k: np.asarray(v) for k, v in inputs.items()}
    x = np.ascontiguousarray(g["L0_attn_x"], dtype=np.float32)
    wq = g["L0_attn_wq"]; wk = g["L0_attn_wk"]; wv = g["L0_attn_wv"]
    wg = g["L0_attn_wg"]; wo = g["L0_attn_wo"]; wd = g["L0_attn_wd"]
    bq = g["L0_attn_bq"]; bk = g["L0_attn_bk"]; bv = g["L0_attn_bv"]
    bgv = g["L0_attn_bg"]; bo = g["L0_attn_bo"]; bd = g["L0_attn_bd"]
    gam = g["L0_attn_gamma"]; bet = g["L0_attn_beta"]
    rw = g["L0_ffn_router_w"]; rb = g["L0_ffn_router_b"]
    w1 = g["L0_ffn_w1"]; b1 = g["L0_ffn_b1"]; w2 = g["L0_ffn_w2"]; b2 = g["L0_ffn_b2"]

    def perm_qk(w):
        wt = np.empty((E, E), np.float32)
        for q in range(4):
            for j in range(4):
                h = 4 * q + j
                re = 64 * h + 2 * np.arange(32)
                wt[:, 256 * q + 32 * j:256 * q + 32 * j + 32] = w[re, :].T
                wt[:, 256 * q + 128 + 32 * j:256 * q + 128 + 32 * j + 32] = w[re + 1, :].T
        return np.ascontiguousarray(wt)

    def col_pack(v):
        return np.ascontiguousarray(v.reshape(-1, 128).T)

    def bias_eo(bvec):
        eE = np.zeros((128, 4), np.float32)
        eO = np.zeros((128, 4), np.float32)
        for q in range(4):
            for j in range(4):
                h = 4 * q + j
                eE[32 * j:32 * j + 32, q] = bvec[64 * h + 2 * np.arange(32)]
                eO[32 * j:32 * j + 32, q] = bvec[64 * h + 2 * np.arange(32) + 1]
        return eE, eO

    bqE, bqO = bias_eo(bq)
    bkE, bkO = bias_eo(bk)

    t_ar = np.arange(T, dtype=np.float64)
    invf = 1.0 / (10000.0 ** (np.arange(0, HD, 2, dtype=np.float64) / HD))
    d_e = 2 * np.arange(32)
    th = lambda dims: t_ar[None, :] * invf[(dims % 32)][:, None]
    cosE = np.tile(np.cos(th(d_e)), (4, 1))
    nsinE = np.tile(-np.sin(th(d_e)), (4, 1))
    cosO = np.tile(np.cos(th(d_e + 1)), (4, 1))
    sinO = np.tile(np.sin(th(d_e + 1)), (4, 1))
    ropetab = np.concatenate([cosE, nsinE, cosO, sinO], axis=1).astype(ml_dtypes.bfloat16)

    dj = np.arange(128)[:, None]
    di = np.arange(512)[None, :]
    tdelta = np.concatenate(
        [np.maximum(d * 128 + dj - di, 0).astype(np.float32) for d in range(4)], axis=1)
    iotaq = np.tile(np.arange(512, dtype=np.float32)[None, :], (16, 1))
    nDELTA = max(T // 128 - 4, 1)
    iota28 = (np.tile((512.0 + 128.0 * np.arange(nDELTA, dtype=np.float32))[None, :],
                      (128, 1)) + np.arange(128, dtype=np.float32)[:, None])
    ind16 = np.zeros((16, 16 * 128), np.float32)
    for h in range(16):
        ind16[h, h * 128:(h + 1) * 128] = 1.0
    hm2 = np.zeros((128, 2), np.float32)
    hm2[0:64, 0] = 1.0
    hm2[64:128, 1] = 1.0
    brows = np.zeros((1, 4 * E), np.float32)
    brows[0, 0:E] = bv; brows[0, E:2 * E] = bo; brows[0, 2 * E:3 * E] = b2
    rowsm = np.zeros((1, 32), np.float32)
    rowsm[0, 0:NBLK] = rb; rowsm[0, NBLK:NBLK + H] = bd

    common = dict(
        wqT=perm_qk(wq), wkT=perm_qk(wk),
        wvT=np.ascontiguousarray(wv.T), wgT=np.ascontiguousarray(wg.T),
        woT=np.ascontiguousarray(wo.T),
        w1T=np.ascontiguousarray(w1.T), w2T=np.ascontiguousarray(w2.T),
        routT=np.ascontiguousarray(rw.T), wdT=np.ascontiguousarray(wd.T),
        bqE=bqE, bqO=bqO, bkE=bkE, bkO=bkO,
        brows=brows,
        bgc=col_pack(bgv), b1c=col_pack(b1), gamc=col_pack(gam), betc=col_pack(bet),
        rowsm=rowsm, bdc=np.ascontiguousarray(bd[:, None].astype(np.float32)),
        rbc=np.ascontiguousarray(rb[:, None].astype(np.float32)),
        identF=np.eye(128, dtype=np.float32),
        ropetab=ropetab, tdelta=tdelta, iotaq=iotaq, iota28=iota28,
        ind16=ind16.astype(ml_dtypes.bfloat16), hm2=hm2,
        hm2T=np.ascontiguousarray(hm2.T),
        onesf=np.ones((1, 128), np.float32),
        constf=np.tile(np.array([[1e-5, -5.0 * LN2, -C_SHIFT]], np.float32),
                       (128, 1)),
    )
    nb = x.shape[0]
    in_maps = []
    for c in range(ncores):
        m = dict(common)
        m["x"] = np.ascontiguousarray(x[p2c * c:p2c * (c + 1)])
        in_maps.append(m)
    return in_maps


def kernel(**inputs):
    in_maps = _host_prep(inputs)
    nc = build_program(P2C=2, T=4096, paranoid="declared")
    res = run_bass_kernel_spmd(nc, in_maps, core_ids=list(range(NCORES)))
    outs = [np.asarray(r["out"]) for r in res.results]
    return np.concatenate(outs, axis=0)
